# version 01 (final)
# baseline (speedup 1.0000x reference)
import contextlib
import numpy as np
import concourse.bass as bass
import concourse.mybir as mybir

F32 = mybir.dt.float32
BF16 = mybir.dt.bfloat16
AF = mybir.ActivationFunctionType
ALU = mybir.AluOpType
AX = mybir.AxisListType.X

EMB = 2048
T = 2048
INTER = 4096
CONVD = 6144
PROJ = 10304
NH, HP, NS, NG, CSZ = 64, 64, 128, 8, 256
PF = 8192
SEG = 512
NSEG = T // SEG
PI = float(np.pi)
KT = EMB // 128
DBG_NO_PSB = False
DBG_NO_APPLY = False
DBG_SKIP_FOLDS = False
DBG_TRIG = 0
DBG_RACE_DETECT = False
DBG_NFOLD = 2
DBG_NO_TDMA = False
DBG_FOLD_LV = 0
DBG_PATHFOLD = True
DBG_MLV = 4


class SharedSem:
    __slots__ = ("h", "n")

    def __init__(self, stack, nc, name):
        self.h = stack.enter_context(nc.semaphore(name))
        self.n = 0


class Sem:
    __slots__ = ("s", "h", "base", "n")

    def __init__(self, shared):
        self.s = shared
        self.h = shared.h
        self.base = shared.n
        self.n = 0

    def inc(self, instr, k=1):
        instr.then_inc(self.h, k)
        self.s.n += k
        self.n += k
        return instr


class Rot:

    def __init__(self, mk, name, k, sw=False):
        self.sems = [mk.dsem(f"{name}_{i}", sw=sw) for i in range(k)]
        self.k = k

    def inc(self, instr, i):
        return self.sems[i % self.k].inc(instr, 16)

    def wait(self, eng, i):
        s = self.sems[i % self.k]
        eng.wait_ge(s.h, s.base + 16 * (i // self.k + 1))


class AbsSem:
    __slots__ = ("h", "base")

    def __init__(self, h):
        self.h = h
        self.base = 0


class MK:
    def __init__(self, upto="full", dump=None):
        self.upto = upto
        self.dump = dump
        self.eps_ln = 1e-5
        self.eps_rms = 1e-6
        self._dumped = False
        self._final_waits = []
        self.phase_gate = []
        nc = bass.Bass("TRN2", target_bir_lowering=False, debug=False,
                       detect_race_conditions=DBG_RACE_DETECT)
        self.nc = nc
        self._dram()

    def _dram(self):
        nc = self.nc
        d = {}
        def ein(name, shape):
            d[name] = nc.dram_tensor(name, shape, F32, kind="ExternalInput").ap()
        ein("x", [T, EMB])
        ein("inproj_w", [PROJ, EMB])
        ein("conv_w", [CONVD, 4]); ein("conv_b", [CONVD])
        ein("dt_bias", [NH]); ein("A_log", [NH]); ein("Dp", [NH])
        ein("norm_w", [INTER])
        ein("outproj_w", [EMB, INTER])
        ein("amp_w", [EMB, EMB]); ein("amp_b", [EMB])
        ein("ph_w", [EMB, EMB]); ein("ph_b", [EMB])
        ein("ent_w", [2, EMB, PF]); ein("ent_b", [2, PF])
        ein("ent_pw", [2, PF, EMB]); ein("ent_pb", [2, EMB])
        ein("path_w", [2, 2, EMB, EMB]); ein("path_b", [2, 2, EMB])
        ein("phase", [2, 2, EMB])
        ein("ln3_g", [2, EMB]); ein("ln3_b", [2, EMB])
        ein("ln4_g", [2, EMB]); ein("ln4_b", [2, EMB])
        self.inp = d
        self.out = nc.dram_tensor("out", [T, EMB], F32, kind="ExternalOutput").ap()
        self.dbg = nc.dram_tensor("dbg", [2048, 2048], F32, kind="ExternalOutput").ap()
        scr = lambda name, shape, dt=BF16: nc.dram_tensor(name, shape, dt, kind="Internal").ap()
        self.wb_ampph = scr("wb_ampph", [2, EMB, EMB])
        self.wb_inproj = scr("wb_inproj", [PROJ + 64, EMB])
        self.wb_outproj = scr("wb_outproj", [EMB, INTER])
        self.wb_entw = scr("wb_entw", [2, EMB, PF])
        self.wb_entpw = scr("wb_entpw", [2, PF, EMB])
        self.wb_fold = scr("wb_fold", [2, EMB, EMB])
        self.wb_pf = scr("wb_pf", [2, EMB, EMB])
        self.x1 = scr("x1", [T, EMB], F32)
        self.x2 = scr("x2", [T, EMB], F32)
        self.rows_scr = scr("rows_scr", [16, EMB], F32)
        self.u2scr = scr("u2scr", [16, 128, T])

    def sem(self, name, persistent=False):
        if persistent:
            sh = SharedSem(self.stack, self.nc, name)
            return Sem(sh)
        if not hasattr(self, "_pool"):
            self._pool = [SharedSem(self.stack, self.nc, f"pool{i}") for i in range(56)]
            self._pool_i = 0
        sh = self._pool[self._pool_i % 56]
        self._pool_i += 1
        return Sem(sh)

    def dsem(self, name, sw=False):
        if not hasattr(self, "_dpool"):
            self._dpool = [SharedSem(self.stack, self.nc, f"dma{i}") for i in range(18)]
            self._dpool_i = 0
            self._spool = [SharedSem(self.stack, self.nc, f"sdma{i}") for i in range(6)]
            self._spool_i = 0
            self._dma_shared = list(self._dpool) + list(self._spool)
        if sw:
            sh = self._spool[self._spool_i % 6]
            self._spool_i += 1
        else:
            sh = self._dpool[self._dpool_i % 18]
            self._dpool_i += 1
        return Sem(sh)

    def sfence(self, s_engine, sem, k=1):
        s_engine.copy(out=self.fence_scr[:, 512:1024], in_=self.fence_scr[:, 0:512])
        s_engine.copy(out=self.fence_scr[:, 0:512], in_=self.fence_scr[:, 512:1024])
        return sem.inc(s_engine.copy(out=self.fence_scr[:, 512:640], in_=self.fence_scr[:, 0:128]), k)

    def gate(self, eng):
        for s, t in getattr(self, "phase_gate", []):
            eng.wait_ge(s.h, s.base + (t))

    def set_gate(self, *pairs):
        q = [(AbsSem(sh.h), sh.n) for sh in getattr(self, "_dma_shared", []) if sh.n > 0]
        self.phase_gate = list(pairs) + q

    @staticmethod
    def bc(t, col_elems, n, nparts=128):
        import math
        pstride = 1
        for s in t.shape[1:]:
            pstride *= s
        return bass.AP(t, col_elems, [[pstride, nparts], [0, n]])

    def build(self):
        nc = self.nc
        self.stack = contextlib.ExitStack()
        with self.stack as st:
            self.ident = st.enter_context(nc.sbuf_tensor("ident", [128, 128], BF16))
            self.ident32 = st.enter_context(nc.sbuf_tensor("ident32", [128, 128], F32))
            self.ones1_32 = st.enter_context(nc.sbuf_tensor("ones1_32", [1, 1024], F32))
            self.tri_u32 = st.enter_context(nc.sbuf_tensor("tri_u32", [128, 128], F32))
            self.tril01 = [st.enter_context(nc.sbuf_tensor(f"tril01_{jc}", [128, 256], F32)) for jc in range(2)]
            self.ccol = st.enter_context(nc.sbuf_tensor("ccol", [128, 8], F32))
            self.ones_col16 = st.enter_context(nc.sbuf_tensor("ones_col16", [128, 1], BF16))
            self.fence_scr = st.enter_context(nc.sbuf_tensor("fence_scr", [128, 1024], F32))
            with nc.Block() as block:
                self.block = block
                self.p0()
                if self.dump == "wentw0_p0":
                    cs, ct = self.cast_done["entw0"]
                    self.dump_hbm_bf16(self.wb_entw[0][:, 0:2048], (cs, ct))
                    self.finish()
                    return nc
                done = self._phase_after("p0")
                if not done:
                    if DBG_SKIP_FOLDS:
                        done = False
                    else:
                        done = self.folds() or self._phase_after("folds")
                if not done:
                    done = self.ffn(0, self.inp["x"], self.x1) or self._phase_after("ffn1")
                if not done:
                    done = self.mamba(self.x1, self.x2) or self._phase_after("mamba")
                if not done:
                    self.ffn(1, self.x2, self.out)
                self.finish()
        return nc

    def _phase_after(self, name):
        return self._dumped or self.upto == name

    def p0(self):
        block = self.block
        self.s_const = self.sem("s_const", persistent=True)
        self.s_constd = self.sem("s_constd", persistent=True)
        self._dma_extra = [self.s_constd]
        self.s_cast = self.sem("s_cast", persistent=True)
        self.cast_done = {}
        inp = self.inp

        s_ms = self.sem("s_ms", persistent=True)

        @block.vector
        def _(v):
            v.memset(self.ident[:], 1.0)
            v.memset(self.ident32[:], 1.0)
            v.memset(self.ones1_32[:], 1.0)
            v.memset(self.ones_col16[:], 1.0)
            v.memset(self.fence_scr[:], 0.0)
            v.memset(self.ccol[:, 0:1], 1e-5)
            v.memset(self.ccol[:, 1:2], 1e-6)
            v.memset(self.ccol[:, 2:3], 1.0)
            v.memset(self.ccol[:, 3:4], 0.5)
            v.memset(self.ccol[:, 4:5], PI / 2)
            v.memset(self.tri_u32[:], 0.0)
            v.memset(self.tril01[0][:], 0.0)
            s_ms.inc(v.memset(self.tril01[1][:], 0.0))

        @block.gpsimd
        def _(g):
            g.wait_ge(s_ms.h, s_ms.base + (1))
            g.affine_select(self.ident[:], self.ident[:], pattern=[[-1, 128]],
                            compare_op=ALU.is_equal, fill=0.0, base=0, channel_multiplier=1)
            g.affine_select(self.ident32[:], self.ident32[:], pattern=[[-1, 128]],
                            compare_op=ALU.is_equal, fill=0.0, base=0, channel_multiplier=1)
            g.affine_select(self.tri_u32[:], self.tri_u32[:], pattern=[[-1, 128]],
                            compare_op=ALU.is_gt, fill=1.0, base=0, channel_multiplier=1)
            for jc in range(2):
                ins = g.affine_select(self.tril01[jc][:], self.tril01[jc][:],
                                      pattern=[[-1, 256]], compare_op=ALU.is_gt, fill=1.0,
                                      base=jc * 128, channel_multiplier=1)
            self.s_const.inc(ins)
            self.s_constd.inc(g.dma_start(out=self.rows_scr[4:5, 0:1024], in_=self.ones1_32[:]), 16)
            for pz in range(2):
                self.s_constd.inc(g.dma_start(out=self.wb_inproj[PROJ + pz * 32:PROJ + (pz + 1) * 32, :],
                                              in_=self.fence_scr[0:64, 0:1024]), 16)
            def cast(name, dst, src, pieces=1):
                sm = self.sem(f"scast_{name}", persistent=True)
                self._dma_extra.append(sm)
                n = src.shape[0]
                step = n // pieces
                for i in range(pieces):
                    sm.inc(g.dma_start(out=dst[i*step:(i+1)*step], in_=src[i*step:(i+1)*step]), 16)
                self.cast_done[name] = (sm, sm.n)
            cast("ampph0", self.wb_ampph[0], inp["amp_w"], 2)
            cast("ampph1", self.wb_ampph[1], inp["ph_w"], 2)
            cast("entw0", self.wb_entw[0], inp["ent_w"][0], 4)
            cast("entpw0", self.wb_entpw[0], inp["ent_pw"][0], 8)
            cast("entw1", self.wb_entw[1], inp["ent_w"][1], 4)
            cast("entpw1", self.wb_entpw[1], inp["ent_pw"][1], 8)
            cast("inproj", self.wb_inproj, inp["inproj_w"], 8)
            cast("outproj", self.wb_outproj, inp["outproj_w"], 2)

    def folds(self):
        nc, block = self.nc, self.block
        self.fold_done = {}
        for l in range(DBG_NFOLD):
            with contextlib.ExitStack() as ph:
                ewT = ph.enter_context(nc.sbuf_tensor(f"ewT_{l}", [128, 64, 512], BF16, is_transpose_dest=True))
                pwS = ph.enter_context(nc.sbuf_tensor(f"pwS_{l}", [128, 4, 512], BF16))
                ebc = ph.enter_context(nc.sbuf_tensor(f"ebc_{l}", [128, 64], BF16))
                ev = ph.enter_context(nc.sbuf_tensor(f"ev_{l}", [128, 2, 4, 512], BF16))
                brow = ph.enter_context(nc.sbuf_tensor(f"brow_{l}", [1, 3, EMB], F32))
                psF = ph.enter_context(nc.psum_tensor(f"psF_{l}", [128, 4, 512], F32))
                psB = ph.enter_context(nc.psum_tensor(f"psB_{l}", [1, 512], F32))
                r_wt = [self.dsem(f"sfw_{l}_0"), self.dsem(f"sfw_{l}_1")]
                r_pw = Rot(self, f"sfp_{l}", 4)
                s_eb = self.dsem(f"sfeb_{l}", sw=True)
                s_mmf = self.sem(f"sfmf_{l}")
                s_evc = self.sem(f"sfev_{l}")
                s_bev = self.sem(f"sfbe_{l}")
                r_st = [self.dsem(f"sfst_{l}_0"), self.dsem(f"sfst_{l}_1")]
                s_st = self.dsem(f"sfst_{l}")
                s_br = self.sem(f"sfbr_{l}")

                @block.gpsimd
                def _(g):
                    self.gate(g)
                    for ft in range(64):
                        g.dma_start(out=ebc[:, ft:ft + 1],
                                    in_=self.inp["ent_b"][l, ft * 128:(ft + 1) * 128].rearrange("(a b) -> a b", b=1)).then_inc(s_eb.h, 16)
                    g.dma_start(out=brow[0:1, 2, :], in_=self.inp["ent_pb"][l:l + 1, :]).then_inc(s_eb.h, 16)
                    s_eb.s.n += 16 * 65
                    s_eb.n += 16 * 65

                @block.sync
                def _(sy):
                    self.gate(sy)
                    cs, ct = self.cast_done[f"entw{l}"]
                    sy.wait_ge(cs.h, cs.base + (ct))
                    for db in range(4):
                        if db >= 1 and DBG_FOLD_LV < 2:
                            sy.wait_ge(s_mmf.h, s_mmf.base + (256 * db))
                        wt = r_wt[db % 2]
                        for ft in range(64):
                            if DBG_NO_TDMA:
                                sy.dma_start(out=ewT[:, ft, :],
                                             in_=self.wb_entw[l][0:128, 0:512]).then_inc(wt.h, 16)
                            else:
                                sy.dma_start(out=ewT[:, ft, :],
                                             in_=self.wb_entw[l][db * 512:(db + 1) * 512, ft * 128:(ft + 1) * 128],
                                             transpose=True).then_inc(wt.h, 16)
                        wt.s.n += 16 * 64
                        if DBG_FOLD_LV >= 1:
                            continue
                        for oq in range(4):
                            g = db * 4 + oq
                            sy.wait_ge(s_evc.h, s_evc.base + (4 * (g + 1)))
                            st = r_st[g % 2]
                            for dt in range(4):
                                d0 = db * 512 + dt * 128
                                st.inc(sy.dma_start(out=self.wb_fold[l][d0:d0 + 128, oq * 512:(oq + 1) * 512],
                                                    in_=ev[:, g % 2, dt, :]), 16)
                    if DBG_FOLD_LV == 0:
                        sy.wait_ge(s_br.h, s_br.base + (1))
                        s_st.inc(sy.dma_start(out=self.rows_scr[l:l + 1, :], in_=brow[0:1, 1, :]), 16)

                @block.scalar
                def _(s):
                    sy = s
                    self.gate(s)
                    cs, ct = self.cast_done[f"entpw{l}"]
                    sy.wait_ge(cs.h, cs.base + (ct))
                    n = 0
                    for db in range(4):
                        for oq in range(4):
                            for ft in range(64):
                                if n >= 4 and DBG_FOLD_LV < 2:
                                    sy.wait_ge(s_mmf.h, s_mmf.base + (n - 3))
                                r_pw.inc(sy.dma_start(out=pwS[:, n % 4, :],
                                             in_=self.wb_entpw[l][ft * 128:(ft + 1) * 128, oq * 512:(oq + 1) * 512]), n)
                                n += 1

                @block.tensor
                def _(te):
                    self.gate(te)
                    if DBG_FOLD_LV >= 2:
                        te.wait_ge(s_eb.h, s_eb.base + (16 * 65))
                        for db in range(4):
                            wt = r_wt[db % 2]
                            te.wait_ge(wt.h, wt.base + (16 * 64 * (db // 2 + 1)))
                        n_tot = 4 * 4 * 64
                        for n in range(n_tot):
                            pass
                        r_pw.wait(te, n_tot - 1)
                        return
                    te.wait_ge(s_eb.h, s_eb.base + (16 * 65))
                    n = 0
                    for db in range(4):
                        wt = r_wt[db % 2]
                        te.wait_ge(wt.h, wt.base + (16 * 64 * (db // 2 + 1)))
                        for oq in range(4):
                            g = db * 4 + oq
                            if g >= 1:
                                te.wait_ge(s_evc.h, s_evc.base + (4 * g))
                            if db == 0 and oq >= 1:
                                te.wait_ge(s_bev.h, s_bev.base + (oq))
                            for ft in range(64):
                                r_pw.wait(te, n)
                                for dt in range(4):
                                    ins = te.matmul(psF[:, dt, :], ewT[:, ft, dt * 128:(dt + 1) * 128],
                                                    pwS[:, n % 4, :], start=(ft == 0), stop=(ft == 63))
                                if db == 0 and not DBG_NO_PSB:
                                    ins = te.matmul(psB[:], ebc[:, ft:ft + 1], pwS[:, n % 4, :],
                                                    start=(ft == 0), stop=(ft == 63))
                                s_mmf.inc(ins)
                                n += 1

                @block.vector
                def _(v):
                    self.gate(v)
                    if DBG_FOLD_LV >= 2:
                        return
                    for db in range(4):
                        for oq in range(4):
                            g = db * 4 + oq
                            v.wait_ge(s_mmf.h, s_mmf.base + (64 * (g + 1)))
                            if g >= 2 and DBG_FOLD_LV == 0:
                                st = r_st[g % 2]
                                v.wait_ge(st.h, st.base + (64 * (g // 2)))
                            for dt in range(4):
                                s_evc.inc(v.tensor_copy(out=ev[:, g % 2, dt, :], in_=psF[:, dt, :]))
                            if db == 0:
                                s_bev.inc(v.tensor_copy(out=brow[0:1, 0, oq * 512:(oq + 1) * 512], in_=psB[:]))
                    v.wait_ge(s_bev.h, s_bev.base + (4))
                    v.wait_ge(s_eb.h, s_eb.base + (16 * 65))
                    s_br.inc(v.tensor_add(out=brow[0:1, 1, :], in0=brow[0:1, 0, :], in1=brow[0:1, 2, :]))

                if DBG_FOLD_LV == 0:
                    self.fold_done[f"ent{l}"] = [(r_st[0], r_st[0].s.n - r_st[0].base),
                                                 (r_st[1], r_st[1].s.n - r_st[1].base),
                                                 (s_st, s_st.n)]
                elif DBG_FOLD_LV == 1:
                    self.fold_done[f"ent{l}"] = [(s_evc, s_evc.n), (s_br, s_br.n)]
                else:
                    self.fold_done[f"ent{l}"] = [(r_wt[0], r_wt[0].s.n - r_wt[0].base),
                                                 (r_wt[1], r_wt[1].s.n - r_wt[1].base),
                                                 (r_pw.sems[3], r_pw.sems[3].s.n - r_pw.sems[3].base),
                                                 (s_eb, s_eb.n)]
                self.set_gate(*self.fold_done[f"ent{l}"])

        if not DBG_PATHFOLD:
            return False
        with contextlib.ExitStack() as ph:
            prow = ph.enter_context(nc.sbuf_tensor("prow", [1, 4, EMB], F32))
            mrow = ph.enter_context(nc.sbuf_tensor("mrow", [1, 4, EMB], F32))
            mrow16 = ph.enter_context(nc.sbuf_tensor("mrow16", [1, 4, EMB], BF16))
            pbrow = ph.enter_context(nc.sbuf_tensor("pbrow", [1, 5, EMB], F32))
            modb = ph.enter_context(nc.sbuf_tensor("modb", [128, 4, EMB], BF16))
            pwt = ph.enter_context(nc.sbuf_tensor("pwt", [128, 4, EMB], BF16))
            wout = ph.enter_context(nc.sbuf_tensor("wout", [128, 2, EMB], BF16))
            ones1b = ph.enter_context(nc.sbuf_tensor("ones1b", [1, 128], BF16))
            psM = ph.enter_context(nc.psum_tensor("psM", [128, 2, EMB], F32))
            s_pr = self.dsem("s_pr")
            s_frac = self.sem("s_frac")
            s_md = self.sem("s_md")
            s_sq2 = self.sem("s_sq2")
            s_mbp = self.sem("s_mbp")
            s_mb = self.sem("s_mb")
            s_pw2a = self.dsem("s_pw2a", sw=True)
            s_pw2b = self.dsem("s_pw2b", sw=True)
            s_wo = self.sem("s_wo")
            s_ws = self.dsem("s_ws")
            s_pb = self.dsem("s_pb")
            MAGIC = float(1.5 * 2 ** 23)

            @block.sync
            def _(sy):
                self.gate(sy)
                for l in range(2):
                    for p in range(2):
                        sy.dma_start(out=prow[0:1, l * 2 + p, :], in_=self.inp["phase"][l, p:p + 1, :]).then_inc(s_pr.h, 16)
                        sy.dma_start(out=pbrow[0:1, l * 2 + p, :], in_=self.inp["path_b"][l, p:p + 1, :]).then_inc(s_pr.h, 16)
                s_pr.s.n += 16 * 8
                s_pr.n += 16 * 8

            @block.vector
            def _(v):
                self.gate(v)
                v.wait_ge(s_pr.h, s_pr.base + (16 * 8))
                for r in range(4):
                    v.tensor_scalar(out=mrow[0:1, r, :], in0=prow[0:1, r, :],
                                    scalar1=1.0 / (2 * PI), scalar2=0.125, op0=ALU.mult, op1=ALU.add)
                    v.tensor_scalar_add(out=prow[0:1, r, :], in0=mrow[0:1, r, :], scalar1=MAGIC)
                    v.tensor_scalar_sub(out=prow[0:1, r, :], in0=prow[0:1, r, :], scalar1=MAGIC)
                    ins = v.tensor_sub(out=mrow[0:1, r, :], in0=mrow[0:1, r, :], in1=prow[0:1, r, :])
                s_frac.inc(ins)

            @block.scalar
            def _(s):
                self.gate(s)
                s.wait_ge(s_frac.h, s_frac.base + (1))
                for r in range(4):
                    ins = s.activation(out=mrow[0:1, r, :], in_=mrow[0:1, r, :], func=AF.Sin, scale=2 * PI)
                s_md.inc(ins)

            @block.vector
            def _(v):
                v.wait_ge(s_md.h, s_md.base + (1))
                for r in range(4):
                    v.tensor_scalar_mul(out=mrow[0:1, r, :], in0=mrow[0:1, r, :], scalar1=float(np.sqrt(2.0)))
                    v.tensor_copy(out=mrow16[0:1, r, :], in_=mrow[0:1, r, :])
                for l in range(2):
                    v.tensor_mul(out=pbrow[0:1, 4, :], in0=pbrow[0:1, 2 * l, :], in1=mrow[0:1, 2 * l, :])
                    v.tensor_mul(out=pbrow[0:1, 2 * l, :], in0=pbrow[0:1, 2 * l + 1, :], in1=mrow[0:1, 2 * l + 1, :])
                    ins = v.tensor_add(out=pbrow[0:1, 2 * l, :], in0=pbrow[0:1, 4, :], in1=pbrow[0:1, 2 * l, :])
                s_sq2.inc(ins)

            @block.gpsimd
            def _(g):
                self.gate(g)
                g.memset(ones1b[:], 1.0)
                n = 0
                for l in range(2):
                    for dt in range(KT):
                        if n >= 2:
                            g.wait_ge(s_wo.h, s_wo.base + (n - 1))
                        sp = s_pw2a if n % 2 == 0 else s_pw2b
                        g.dma_start(out=pwt[:, (n % 2) * 2, :],
                                    in_=self.inp["path_w"][l, 0, dt * 128:(dt + 1) * 128, :]).then_inc(sp.h, 16)
                        g.dma_start(out=pwt[:, (n % 2) * 2 + 1, :],
                                    in_=self.inp["path_w"][l, 1, dt * 128:(dt + 1) * 128, :]).then_inc(sp.h, 16)
                        sp.s.n += 32
                        n += 1

            @block.tensor
            def _(te):
                self.gate(te)
                te.wait_ge(s_sq2.h, s_sq2.base + (1))
                for r in range(4):
                    if r >= 2:
                        te.wait_ge(s_mb.h, s_mb.base + (r - 1))
                    for c in range(4):
                        ins = te.matmul(psM[:, r % 2, c * 512:(c + 1) * 512], ones1b[:],
                                        mrow16[0:1, r, c * 512:(c + 1) * 512], start=True, stop=True)
                    s_mbp.inc(ins)

            @block.scalar
            def _(s):
                for r in range(4):
                    s.wait_ge(s_mbp.h, s_mbp.base + (r + 1))
                    s.copy(out=modb[:, r, :], in_=psM[:, r % 2, :])
                    self.sfence(s, s_mb)

            @block.vector
            def _(v):
                v.wait_ge(s_mb.h, s_mb.base + (4))
                n = 0
                for l in range(2):
                    for dt in range(KT):
                        sp = s_pw2a if n % 2 == 0 else s_pw2b
                        v.wait_ge(sp.h, sp.base + (32 * (n // 2 + 1)))
                        if n >= 2:
                            v.wait_ge(s_ws.h, s_ws.base + (16 * (n - 1)))
                        b = n % 2
                        v.tensor_mul(out=pwt[:, b * 2, :], in0=pwt[:, b * 2, :], in1=modb[:, l * 2, :])
                        v.tensor_mul(out=pwt[:, b * 2 + 1, :], in0=pwt[:, b * 2 + 1, :], in1=modb[:, l * 2 + 1, :])
                        s_wo.inc(v.tensor_add(out=wout[:, b, :], in0=pwt[:, b * 2, :], in1=pwt[:, b * 2 + 1, :]))
                        n += 1

            @block.scalar
            def _(s):
                n = 0
                for l in range(2):
                    for dt in range(KT):
                        s.wait_ge(s_wo.h, s_wo.base + (n + 1))
                        s_ws.inc(s.dma_start(out=self.wb_pf[l][dt * 128:(dt + 1) * 128, :],
                                             in_=wout[:, n % 2, :]), 16)
                        n += 1
                s.wait_ge(s_sq2.h, s_sq2.base + (1))
                for l in range(2):
                    s_pb.inc(s.dma_start(out=self.rows_scr[2 + l:3 + l, :], in_=pbrow[0:1, 2 * l, :]), 16)

            self.fold_done["path"] = (s_ws, s_ws.n)
            self.fold_done["pb"] = (s_pb, s_pb.n)
            self.set_gate((s_ws, s_ws.n), (s_pb, s_pb.n), (s_mb, s_mb.n))
        if self.dump == "wentw0":
            cs, ct = self.cast_done["entw0"]
            self.dump_hbm_bf16(self.wb_entw[0][:, 0:2048], (cs, ct))
            return True
        if self.dump == "wfold0":
            self.dump_hbm_bf16(self.wb_fold[0], self.fold_done["ent0"])
            return True
        if self.dump == "wpf0":
            self.dump_hbm_bf16(self.wb_pf[0], self.fold_done["path"])
            return True
        return False

    def ffn(self, inv, src, dst):
        nc = self.nc
        with contextlib.ExitStack() as fstack:
            self.PING = fstack.enter_context(nc.sbuf_tensor(f"PING{inv}", [128, KT, T], BF16))
            self.PONG = fstack.enter_context(nc.sbuf_tensor(f"PONG{inv}", [128, KT, T], BF16))
            lnb = self.ln_boundary(f"f{inv}", src, self.PING)
            if self._dumped:
                return True
            if self.dump == f"u{inv}":
                self.dump_fm(self.PING, lnb)
                return True
            if self.dump == "pingafter":
                hdone = self.ffn_ampph(inv, lnb)
                self.dump_fm(self.PING, hdone)
                return True
            hdone = self.ffn_ampph(inv, lnb)
            if self.dump == "pingafter":
                self.dump_fm(self.PING, hdone)
                return True
            if self.dump in (f"h{inv}", f"amp{inv}", "rawmm"):
                self.dump_fm(self.PONG, hdone)
                return True
            cur, oth = self.PONG, self.PING
            for l in range(2):
                hdone = self.ffn_stage(inv, l, "ent", cur, oth, hdone)
                if self.dump == f"hent{l}{inv}":
                    self.dump_fm(oth, hdone)
                    return True
                cur, oth = oth, cur
                hdone = self.ffn_stage(inv, l, "path", cur, oth, hdone)
                if self.dump == f"hpath{l}{inv}":
                    self.dump_fm(oth, hdone)
                    return True
                cur, oth = oth, cur
            rdone = self.ffn_resid(inv, cur, src, dst, hdone)
            if self.dump == ("x1" if inv == 0 else "xout"):
                self.dump_tok(dst, rdone)
                return True
            self.resid_done = rdone
            return False

    def ffn_stage(self, inv, l, kind, hbuf, obuf, prev_done):
        nc, block = self.nc, self.block
        tag = f"{kind}{l}_{inv}"
        psem, ptgt = prev_done
        wsrc = self.wb_fold[l] if kind == "ent" else self.wb_pf[l]
        brow_idx = l if kind == "ent" else 2 + l
        g_name, b_name = ("ln3_g", "ln3_b") if kind == "ent" else ("ln4_g", "ln4_b")
        fold_waits = self.fold_done[f"ent{l}"] if kind == "ent" else [self.fold_done["path"]]
        pb_sem, pb_tgt = self.fold_done["pb"]
        with contextlib.ExitStack() as ph:
            wsb = ph.enter_context(nc.sbuf_tensor(f"wsb_{tag}", [128, 2, KT, 128], BF16))
            bcol = ph.enter_context(nc.sbuf_tensor(f"bcol_{tag}", [128, 16], F32))
            bcol_loc = ph.enter_context(nc.sbuf_tensor(f"bcoll_{tag}", [128, 16], F32))
            vfs = ph.enter_context(nc.sbuf_tensor(f"vfs_{tag}", [128, 512], F32))
            gbs = ph.enter_context(nc.sbuf_tensor(f"gbs_{tag}", [2, EMB], F32))
            rhs2 = ph.enter_context(nc.sbuf_tensor(f"rhs2_{tag}", [2, 1024], F32))
            sqb = ph.enter_context(nc.sbuf_tensor(f"sqb_{tag}", [128, 2, 1024], BF16))
            srow = ph.enter_context(nc.sbuf_tensor(f"srow_{tag}", [1, 7, 1024], F32))
            dbgr = ph.enter_context(nc.sbuf_tensor(f"dbgr_{tag}", [1, 6, 1024], BF16)) if self.dump == "rows_" + tag else None
            psm = ph.enter_context(nc.psum_tensor(f"psm_{tag}", [128, 2, 1024], F32))
            pmul = ph.enter_context(nc.psum_tensor(f"pmul_{tag}", [128, 1024], F32))
            padd = ph.enter_context(nc.psum_tensor(f"padd_{tag}", [128, 1024], F32))
            s_b = self.dsem(f"sb_{tag}")
            r_w = Rot(self, f"sw_{tag}", 2)
            s_mm = self.sem(f"smm_{tag}")
            s_r = self.sem(f"sr_{tag}")
            s_sq = self.sem(f"ssq_{tag}")
            s_sm = self.sem(f"ssm_{tag}")
            s_vr = self.sem(f"svr_{tag}")
            s_row = self.sem(f"srw_{tag}")
            s_row2 = self.sem(f"srw2_{tag}")
            s_bc = self.sem(f"sbc_{tag}")
            s_ap = self.sem(f"sap_{tag}")

            @block.sync
            def _(sy):
                self.gate(sy)
                for fs, ft_ in fold_waits:
                    sy.wait_ge(fs.h, fs.base + (ft_))
                if kind == "path":
                    sy.wait_ge(pb_sem.h, pb_sem.base + (pb_tgt))
                sy.wait_ge(psem.h, psem.base + (ptgt))
                for k in range(KT):
                    s_b.inc(sy.dma_start(out=bcol[:, k:k + 1],
                                         in_=self.rows_scr[brow_idx, k * 128:(k + 1) * 128].rearrange("(a b) -> a b", b=1)), 16)
                s_b.inc(sy.dma_start(out=gbs[0:1, :], in_=self.inp[g_name][l:l + 1, :]), 16)
                s_b.inc(sy.dma_start(out=gbs[1:2, :], in_=self.inp[b_name][l:l + 1, :]), 16)
                sy.wait_ge(self.s_constd.h, self.s_constd.base + (16))
                s_b.inc(sy.dma_start(out=rhs2[1:2, :], in_=self.rows_scr[4:5, 0:1024]), 16)
                n = 0
                for th in range(2):
                    for o in range(KT):
                        if n >= 2:
                            sy.wait_ge(s_mm.h, s_mm.base + (n - 1))
                        r_w.inc(sy.dma_start(out=wsb[:, n % 2],
                                             in_=wsrc.rearrange("(k p) o -> p k o", p=128)[:, :, o * 128:(o + 1) * 128]), n)
                        n += 1

            @block.tensor
            def _(te):
                self.gate(te)
                n = 0
                for th in range(2):
                    if th == 1:
                        te.wait_ge(s_ap.h, s_ap.base + (KT))
                    for o in range(KT):
                        r_w.wait(te, n)
                        if n >= 2:
                            te.wait_ge(s_r.h, s_r.base + (n - 1))
                        for k in range(KT):
                            for hf in range(2):
                                ins = te.matmul(psm[:, n % 2, hf * 512:(hf + 1) * 512],
                                                wsb[:, n % 2, k, :],
                                                hbuf[:, k, th * 1024 + hf * 512: th * 1024 + hf * 512 + 512],
                                                start=(k == 0), stop=(k == KT - 1))
                        s_mm.inc(ins)
                        te.wait_ge(s_r.h, s_r.base + (n + 1))
                        te.wait_ge(s_sq.h, s_sq.base + (n + 1))
                        for hf in range(2):
                            te.matmul(pmul[0:1, hf * 512:(hf + 1) * 512], self.ones_col16[:, 0:1],
                                      obuf[:, o, th * 1024 + hf * 512: th * 1024 + hf * 512 + 512],
                                      start=(o == 0), stop=(o == KT - 1), skip_group_check=True)
                            ins = te.matmul(padd[0:1, hf * 512:(hf + 1) * 512], self.ones_col16[:, 0:1],
                                            sqb[:, n % 2, hf * 512:(hf + 1) * 512],
                                            start=(o == 0), stop=(o == KT - 1), skip_group_check=True)
                        s_sm.inc(ins)
                        n += 1
                    te.wait_ge(s_row.h, s_row.base + (th + 1))
                    te.wait_ge(s_row2.h, s_row2.base + (th + 1))
                    for o in range(KT):
                        for hf in range(2):
                            te.matmul(pmul[:, hf * 512:(hf + 1) * 512], gbs[0:1, o * 128:(o + 1) * 128],
                                      srow[0:1, 4, hf * 512:(hf + 1) * 512], start=True, stop=True,
                                      skip_group_check=True)
                            ins = te.matmul(padd[:, hf * 512:(hf + 1) * 512], gbs[0:2, o * 128:(o + 1) * 128],
                                            rhs2[0:2, hf * 512:(hf + 1) * 512], start=True, stop=True,
                                            skip_group_check=True)
                        s_bc.inc(ins)
                        te.wait_ge(s_ap.h, s_ap.base + (th * KT + o + 1))

            @block.vector
            def _(v):
                self.gate(v)
                v.wait_ge(s_b.h, s_b.base + (16 * (KT + 3)))
                v.memset(vfs[:], 0.0)
                v.tensor_copy(out=bcol_loc[:], in_=bcol[:])
                for _pad in range(3):
                    v.tensor_copy(out=vfs[:, 256:384], in_=vfs[:, 0:128])
                    v.tensor_copy(out=vfs[:, 384:512], in_=vfs[:, 128:256])
                n = 0
                for th in range(2):
                    for o in range(KT):
                        v.wait_ge(s_mm.h, s_mm.base + (n + 1))
                        sl = obuf[:, o, th * 1024:(th + 1) * 1024]
                        s_r.inc(v.scalar_tensor_tensor(out=sl,
                                                       in0=psm[:, n % 2, :], scalar=bcol_loc[:, o:o + 1],
                                                       in1=hbuf[:, o, th * 1024:(th + 1) * 1024],
                                                       op0=ALU.add, op1=ALU.add))
                        if n >= 2:
                            v.wait_ge(s_sm.h, s_sm.base + (n - 1))
                        s_sq.inc(v.tensor_mul(out=sqb[:, n % 2, :], in0=sl, in1=sl))
                        n += 1
                    v.wait_ge(s_sm.h, s_sm.base + (16 * (th + 1)))
                    if self.dump == "rows_" + tag and th == 0:
                        v.tensor_copy(out=dbgr[0:1, 4, :], in_=pmul[0:1, :])
                        v.tensor_copy(out=dbgr[0:1, 5, :], in_=padd[0:1, :])
                    v.tensor_scalar_mul(out=srow[0:1, 2, :], in0=pmul[0:1, :], scalar1=1.0 / EMB)
                    v.tensor_mul(out=srow[0:1, 5, :], in0=srow[0:1, 2, :], in1=srow[0:1, 2, :])
                    v.tensor_scalar_sub(out=srow[0:1, 5, :], in0=srow[0:1, 5, :], scalar1=self.eps_ln)
                    v.scalar_tensor_tensor(out=srow[0:1, 3, :], in0=padd[0:1, :], scalar=1.0 / EMB,
                                           in1=srow[0:1, 5, :], op0=ALU.mult, op1=ALU.subtract)
                    s_vr.inc(v.reciprocal(out=srow[0:1, 3, :], in_=srow[0:1, 3, :]))
                    v.wait_ge(s_row.h, s_row.base + (th + 1))
                    s_row2.inc(v.scalar_tensor_tensor(out=rhs2[0:1, :], in0=srow[0:1, 2, :], scalar=-1.0,
                                                      in1=srow[0:1, 4, :], op0=ALU.mult, op1=ALU.mult))
                    if self.dump == "rows_" + tag and th == 0:
                        v.tensor_copy(out=dbgr[0:1, 0, :], in_=srow[0:1, 2, :])
                        v.tensor_copy(out=dbgr[0:1, 1, :], in_=srow[0:1, 3, :])
                        v.tensor_copy(out=dbgr[0:1, 2, :], in_=srow[0:1, 4, :])
                        v.tensor_copy(out=dbgr[0:1, 3, :], in_=rhs2[0:1, :])
                    for o in range(KT):
                        v.wait_ge(s_bc.h, s_bc.base + (th * KT + o + 1))
                        sl = obuf[:, o, th * 1024:(th + 1) * 1024]
                        if DBG_NO_APPLY:
                            s_ap.inc(v.tensor_copy(out=sqb[:, 0, 0:4], in_=sqb[:, 1, 0:4]))
                        else:
                            v.tensor_tensor(out=sl, in0=sl, in1=pmul[:, :], op=ALU.mult)
                            s_ap.inc(v.tensor_tensor(out=sl, in0=sl, in1=padd[:, :], op=ALU.add))

            @block.scalar
            def _(s):
                self.gate(s)
                for th in range(2):
                    s.wait_ge(s_vr.h, s_vr.base + (th + 1))
                    s.activation(out=srow[0:1, 4, :], in_=srow[0:1, 3, :], func=AF.Sqrt)
                    s_row.inc(s.copy(out=srow[0:1, 6, :], in_=srow[0:1, 4, :]))

            if self.dump == "rows_" + tag:
                s_dr = self.dsem(f"sdr_{tag}", sw=True)

                @block.gpsimd
                def _(g):
                    g.wait_ge(s_ap.h, s_ap.base + (s_ap.n))
                    for q in range(6):
                        s_dr.inc(g.dma_start(out=self.dbg[q:q + 1, 0:1024], in_=dbgr[0:1, q, :]), 16)
                self._final_waits.append((s_dr, s_dr.n))
                self._dumped = True
            self.set_gate((s_ap, s_ap.n))
            return (s_ap, s_ap.n)

    def ffn_resid(self, inv, hbuf, src, dst, prev_done):
        nc, block = self.nc, self.block
        tag = f"res{inv}"
        psem, ptgt = prev_done
        with contextlib.ExitStack() as ph:
            xin = ph.enter_context(nc.sbuf_tensor(f"xr_{tag}", [128, 2, EMB], F32))
            htok = ph.enter_context(nc.sbuf_tensor(f"ht_{tag}", [128, 2, EMB], BF16))
            xo = ph.enter_context(nc.sbuf_tensor(f"xo_{tag}", [128, 2, EMB], F32))
            pstr = ph.enter_context(nc.psum_tensor(f"pstr_{tag}", [128, 2, 1024], BF16))
            r_x = Rot(self, f"sx_{tag}", 2)
            s_tp = self.sem(f"stp_{tag}")
            s_ev = self.sem(f"sev_{tag}")
            s_o = self.sem(f"so_{tag}")
            r_st = Rot(self, f"sst_{tag}", 2, sw=True)
            nt = T // 128

            @block.sync
            def _(sy):
                self.gate(sy)
                for i in range(nt):
                    if i >= 2:
                        sy.wait_ge(s_o.h, s_o.base + (i - 1))
                    r_x.inc(sy.dma_start(out=xin[:, i % 2, :], in_=src[i * 128:(i + 1) * 128, :]), i)

            @block.tensor
            def _(te):
                self.gate(te)
                te.wait_ge(psem.h, psem.base + (ptgt))
                for i in range(nt):
                    for grp in range(2):
                        g = 2 * i + grp
                        if g >= 2:
                            te.wait_ge(s_ev.h, s_ev.base + (g - 1))
                        for j in range(8):
                            k = grp * 8 + j
                            ins = te.transpose(pstr[:, grp, j * 128:(j + 1) * 128],
                                               hbuf[:, k, i * 128:(i + 1) * 128], self.ident[:])
                        s_tp.inc(ins)

            @block.scalar
            def _(s):
                self.gate(s)
                for i in range(nt):
                    for grp in range(2):
                        s.wait_ge(s_tp.h, s_tp.base + (2 * i + grp + 1))
                        if i >= 2:
                            s.wait_ge(s_o.h, s_o.base + (i - 1))
                        dst_sl = htok[:, i % 2, :].rearrange("p (a b) -> p a b", a=2)[:, grp, :]
                        s.copy(out=dst_sl, in_=pstr[:, grp, :])
                        self.sfence(s, s_ev)

            @block.vector
            def _(v):
                self.gate(v)
                for i in range(nt):
                    r_x.wait(v, i)
                    v.wait_ge(s_ev.h, s_ev.base + (2 * (i + 1)))
                    if i >= 2:
                        r_st.wait(v, i - 2)
                    s_o.inc(v.scalar_tensor_tensor(out=xo[:, i % 2, :], in0=htok[:, i % 2, :],
                                                   scalar=0.5, in1=xin[:, i % 2, :],
                                                   op0=ALU.mult, op1=ALU.add))

            @block.gpsimd
            def _(g):
                self.gate(g)
                for i in range(nt):
                    g.wait_ge(s_o.h, s_o.base + (i + 1))
                    r_st.inc(g.dma_start(out=dst[i * 128:(i + 1) * 128, :], in_=xo[:, i % 2, :]), i)

            done = [(r_st.sems[0], r_st.sems[0].s.n - r_st.sems[0].base),
                    (r_st.sems[1], r_st.sems[1].s.n - r_st.sems[1].base)]
            self._final_waits.extend(done)
            self.set_gate(*done)
            return done[0]

    def ln_boundary(self, tag, src, dst_fm, prev=None, dst_hbm=None):
        nc, block = self.nc, self.block
        with contextlib.ExitStack() as ph:
            canary = ph.enter_context(nc.sbuf_tensor(f"can_{tag}", [128, 256], F32))
            mc = ph.enter_context(nc.sbuf_tensor(f"mc_{tag}", [128, 2, 8], F32))
            xin = ph.enter_context(nc.sbuf_tensor(f"xin_{tag}", [128, 2, EMB], F32))
            sqb = ph.enter_context(nc.sbuf_tensor(f"sqb_{tag}", [128, 2, EMB], BF16))
            utok = ph.enter_context(nc.sbuf_tensor(f"utok_{tag}", [128, 2, EMB], BF16))
            r64 = ph.enter_context(nc.sbuf_tensor(f"r64_{tag}", [128, 2, 64], F32))
            vf = ph.enter_context(nc.sbuf_tensor(f"vf_{tag}", [128, 512], F32))
            slk = ph.enter_context(nc.sbuf_tensor(f"slk_{tag}", [128, 512], F32))
            relay_scr = ph.enter_context(nc.sbuf_tensor(f"rls_{tag}", [128, 8], F32))
            dbgU = ph.enter_context(nc.sbuf_tensor(f"dbgU_{tag}", [128, 2, EMB], BF16)) if self.dump == "utok01" else None
            fmb = ph.enter_context(nc.sbuf_tensor(f"fmb_{tag}", [128, 2, 8, 128], BF16)) if dst_hbm is not None else None
            r_us = Rot(self, f"sus_{tag}", 2, sw=True) if dst_hbm is not None else None
            stc = ph.enter_context(nc.sbuf_tensor(f"stc_{tag}", [128, 4, 16], F32)) if self.dump == "lnst" else None
            ps_tr = ph.enter_context(nc.psum_tensor(f"pstr_{tag}", [128, 2, 1024], BF16))
            r_x = Rot(self, f"sx_{tag}", 2)
            s_sq = self.sem(f"ssq_{tag}")
            s_red = self.sem(f"sred_{tag}")
            s_rg = self.sem(f"srg_{tag}")
            s_scrub = self.sem(f"sscrub_{tag}")
            s_vr = self.sem(f"svr_{tag}")
            s_rstd = self.sem(f"srstd_{tag}")
            s_u = self.sem(f"su_{tag}")
            s_tp = self.sem(f"stp_{tag}")
            s_ev = self.sem(f"sev_{tag}")
            nt = T // 128

            @block.sync
            def _(sy):
                self.gate(sy)
                sy.wait_ge(s_scrub.h, s_scrub.base + (1))
                if prev is not None:
                    sy.wait_ge(prev[0].h, prev[0].base + (prev[1]))
                for i in range(nt):
                    if i >= 2:
                        sy.wait_ge(s_u.h, s_u.base + (i - 1))
                    r_x.inc(sy.dma_start(out=xin[:, i % 2, :], in_=src[i * 128:(i + 1) * 128, :]), i)

            @block.scalar
            def _(s):
                self.gate(s)
                s.wait_ge(self.s_const.h, self.s_const.base + (1))
                for i in range(nt + 1):
                    b = i % 2
                    if i < nt:
                        s.wait_ge(s_vr.h, s_vr.base + (i + 1))
                        s.activation(out=r64[:, b, :], in_=self.bc(mc, b * 8 + 6, 64), func=AF.Sqrt)
                        s.copy(out=slk[:, 0:256], in_=slk[:, 256:512])
                        s_rstd.inc(s.copy(out=slk[:, 256:512], in_=slk[:, 0:256]))
                    if i >= 1:
                        j = i - 1
                        for grp in range(2):
                            s.wait_ge(s_tp.h, s_tp.base + (2 * j + grp + 1))
                            if dst_hbm is not None:
                                if 2 * j + grp >= 2:
                                    r_us.wait(s, 2 * j + grp - 2)
                                dst = fmb[:, (2 * j + grp) % 2, :, :]
                            else:
                                dst = dst_fm[:, grp * 8:grp * 8 + 8, j * 128:(j + 1) * 128]
                            s.copy(out=dst, in_=ps_tr[:, grp, :].rearrange("p (a b) -> p a b", a=8))
                            self.sfence(s, s_ev)

            @block.vector
            def _(v):
                self.gate(v)
                v.memset(vf[:], 0.0)

                def vpad():
                    v.tensor_copy(out=vf[:, 256:384], in_=vf[:, 0:128])
                    v.tensor_copy(out=vf[:, 384:512], in_=vf[:, 128:256])
                for i in range(nt):
                    b = i % 2
                    r_x.wait(v, i)
                    v.reduce_sum(out=mc[:, b, 0:1], in_=xin[:, b, :], axis=AX)
                    v.scalar_tensor_tensor(out=sqb[:, b, :], in0=xin[:, b, :],
                                           scalar=1.0, in1=xin[:, b, :],
                                           op0=ALU.mult, op1=ALU.mult,
                                           accum_out=mc[:, b, 2:3])
                    v.tensor_scalar_mul(out=mc[:, b, 1:2], in0=mc[:, b, 0:1], scalar1=1.0 / EMB)
                    vpad()
                    v.tensor_mul(out=mc[:, b, 3:4], in0=mc[:, b, 1:2], in1=mc[:, b, 1:2])
                    vpad()
                    v.scalar_tensor_tensor(out=mc[:, b, 4:5], in0=mc[:, b, 2:3],
                                           scalar=1.0 / EMB, in1=mc[:, b, 3:4],
                                           op0=ALU.mult, op1=ALU.subtract)
                    vpad()
                    v.tensor_scalar_add(out=mc[:, b, 4:5], in0=mc[:, b, 4:5], scalar1=self.eps_ln)
                    vpad()
                    s_vr.inc(v.reciprocal(out=mc[:, b, 6:7], in_=mc[:, b, 4:5]))
                    if stc is not None and i < 4:
                        v.tensor_copy(out=stc[:, i, 0:8], in_=mc[:, b, :])
                    v.wait_ge(s_rg.h, s_rg.base + (i + 1))
                    if i >= 2:
                        v.wait_ge(s_tp.h, s_tp.base + (2 * (i - 1)))
                    if stc is not None and i < 4:
                        v.tensor_copy(out=stc[:, i, 8:9], in_=r64[:, b, 0:1])
                        v.tensor_copy(out=stc[:, i, 9:10], in_=r64[:, b, 32:33])
                    s_u.inc(v.scalar_tensor_tensor(out=utok[:, b, :], in0=xin[:, b, :],
                                                   scalar=mc[:, b, 1:2], in1=self.bc(r64, b * 64, EMB),
                                                   op0=ALU.subtract, op1=ALU.mult))
                    if dbgU is not None and i < 2:
                        v.tensor_copy(out=dbgU[:, i, :], in_=utok[:, b, :])

            def _store(g, q):
                j, grp = q // 2, q % 2
                g.wait_ge(s_ev.h, s_ev.base + (q + 1))
                r_us.inc(g.dma_start(
                    out=dst_hbm[grp * 8:(grp + 1) * 8, :, j * 128:(j + 1) * 128].rearrange("k p t -> p k t"),
                    in_=fmb[:, q % 2, :, :]), q)

            @block.gpsimd
            def _(g):
                self.gate(g)
                g.memset(slk[:], 0.0)
                s_scrub.inc(g.memset(canary[:], 777.0))
                for i in range(nt):
                    g.wait_ge(s_rstd.h, s_rstd.base + (i + 1))
                    s_rg.inc(g.memset(relay_scr[:], 0.0))
                    if dst_hbm is not None and i >= 2:
                        _store(g, 2 * (i - 2))
                        _store(g, 2 * (i - 2) + 1)
                if dst_hbm is not None:
                    for q in range(2 * (nt - 2), 2 * nt):
                        _store(g, q)

            @block.tensor
            def _(te):
                self.gate(te)
                te.wait_ge(self.s_const.h, self.s_const.base + (1))
                for i in range(nt):
                    b = i % 2
                    te.wait_ge(s_u.h, s_u.base + (i + 1))
                    for grp in range(2):
                        if 2 * i + grp >= 2:
                            te.wait_ge(s_ev.h, s_ev.base + (2 * i + grp - 1))
                        for j in range(8):
                            k = grp * 8 + j
                            ins = te.transpose(ps_tr[:, grp, j * 128:(j + 1) * 128],
                                               utok[:, b, k * 128:(k + 1) * 128], self.ident[:])
                        s_tp.inc(ins)

            if self.dump == "canary" and tag == "f0":
                s_dc = self.dsem("s_dc", sw=True)

                @block.gpsimd
                def _(g):
                    g.wait_ge(s_ev.h, s_ev.base + (s_ev.n))
                    s_dc.inc(g.dma_start(out=self.dbg[0:128, 0:256], in_=canary[:]), 16)
                self._final_waits.append((s_dc, s_dc.n))
                self._dumped = True
            if stc is not None:
                s_dl = self.dsem(f"sdl_{tag}", sw=True)

                @block.gpsimd
                def _(g):
                    g.wait_ge(s_u.h, s_u.base + (s_u.n))
                    s_dl.inc(g.dma_start(out=self.dbg[0:128, 0:64], in_=stc[:].rearrange("p a b -> p (a b)")), 16)
                self._final_waits.append((s_dl, s_dl.n))
                self._dumped = True
            if dbgU is not None:
                s_du = self.dsem(f"sdu_{tag}", sw=True)

                @block.gpsimd
                def _(g):
                    g.wait_ge(s_u.h, s_u.base + (s_u.n))
                    s_du.inc(g.dma_start(out=self.dbg[0:128, :], in_=dbgU[:, 0, :]), 16)
                    s_du.inc(g.dma_start(out=self.dbg[128:256, :], in_=dbgU[:, 1, :]), 16)
                self._final_waits.append((s_du, s_du.n))
                self._dumped = True
            if dst_hbm is not None:
                done = [(r_us.sems[0], r_us.sems[0].s.n - r_us.sems[0].base),
                        (r_us.sems[1], r_us.sems[1].s.n - r_us.sems[1].base)]
                self.set_gate(*done, (s_u, s_u.n))
                return done
            self.set_gate((s_ev, s_ev.n), (s_u, s_u.n))
            return (s_ev, s_ev.n)

    def ffn_ampph(self, inv, lnb):
        nc, block = self.nc, self.block
        lnsem, lntgt = lnb
        with contextlib.ExitStack() as ph:
            wsb = ph.enter_context(nc.sbuf_tensor(f"wsb_a{inv}", [128, 2, KT, 128], BF16))
            t1 = ph.enter_context(nc.sbuf_tensor(f"t1_a{inv}", [128, 2, 1024], F32))
            t2 = ph.enter_context(nc.sbuf_tensor(f"t2_a{inv}", [128, 2, 1024], BF16))
            t3 = ph.enter_context(nc.sbuf_tensor(f"t3_a{inv}", [128, 2, 1024], BF16))
            t4 = ph.enter_context(nc.sbuf_tensor(f"t4_a{inv}", [128, 1024], BF16))
            bias = ph.enter_context(nc.sbuf_tensor(f"bias_a{inv}", [128, 2, 16], F32))
            bias_loc = ph.enter_context(nc.sbuf_tensor(f"biasl_a{inv}", [128, 2, 16], F32))
            psm = ph.enter_context(nc.psum_tensor(f"ps_a{inv}", [128, 3, 1024], F32))
            amponly = self.dump in (f"amp{inv}", "rawmm")
            s_b = self.dsem(f"sb_a{inv}")
            r_w = Rot(self, f"sw_a{inv}", 2)
            s_mm = self.sem(f"smm_a{inv}")
            s_ep = self.sem(f"sep_a{inv}")
            s_ph = self.sem(f"sph_a{inv}")

            @block.sync
            def _(sy):
                self.gate(sy)
                for w in range(2):
                    name = "amp_b" if w == 0 else "ph_b"
                    for k in range(KT):
                        s_b.inc(sy.dma_start(out=bias[:, w, k:k + 1],
                                             in_=self.inp[name][k * 128:(k + 1) * 128].rearrange("(a b) -> a b", b=1)), 16)
                for nm in ("ampph0", "ampph1"):
                    cs, ct = self.cast_done[nm]
                    sy.wait_ge(cs.h, cs.base + (ct))
                for w in range(1 if amponly else 2):
                    for o in range(KT):
                        g = w * KT + o
                        if g >= 2:
                            sy.wait_ge(s_ep.h, s_ep.base + (2 * (g - 1)))
                        r_w.inc(sy.dma_start(out=wsb[:, g % 2],
                                             in_=self.wb_ampph.rearrange("w (k p) o -> w p k o", p=128)[w, :, :, o * 128:(o + 1) * 128]), g)

            @block.tensor
            def _(te):
                self.gate(te)
                te.wait_ge(lnsem.h, lnsem.base + (lntgt))
                n = 0
                for w in range(1 if amponly else 2):
                    for o in range(KT):
                        g = w * KT + o
                        r_w.wait(te, g)
                        for th in range(2):
                            if n >= 3:
                                te.wait_ge(s_ep.h, s_ep.base + (n - 2))
                            for k in range(KT):
                                for hf in range(2):
                                    ins = te.matmul(psm[:, n % 3, hf * 512:(hf + 1) * 512],
                                                    wsb[:, g % 2, k, :],
                                                    self.PING[:, k, th * 1024 + hf * 512: th * 1024 + hf * 512 + 512],
                                                    start=(k == 0), stop=(k == KT - 1))
                            s_mm.inc(ins)
                            n += 1

            @block.scalar
            def _(s):
                self.gate(s)
                s.wait_ge(s_b.h, s_b.base + (16 * 32))
                s.copy(out=bias_loc[:, 0, :], in_=bias[:, 0, :])
                s.copy(out=bias_loc[:, 1, :], in_=bias[:, 1, :])
                for _pad in range(3):
                    s.copy(out=self.fence_scr[:, 512:1024], in_=self.fence_scr[:, 0:512])
                n = 0
                for o in range(KT):
                    for th in range(2):
                        s.wait_ge(s_mm.h, s_mm.base + (n + 1))
                        if self.dump == "rawmm":
                            s.copy(out=self.PONG[:, o, th * 1024:(th + 1) * 1024],
                                   in_=psm[:, n % 3, :])
                            self.sfence(s, s_ep)
                        else:
                            s.activation(out=self.PONG[:, o, th * 1024:(th + 1) * 1024],
                                         in_=psm[:, n % 3, :], func=AF.Sigmoid,
                                         bias=bias_loc[:, 0, o:o + 1])
                            self.sfence(s, s_ep)
                        n += 1
                j = 0
                for o in range(0 if amponly else KT):
                    for th in range(2):
                        s.wait_ge(s_mm.h, s_mm.base + (n + 1))
                        if j >= 2:
                            s.wait_ge(s_ph.h, s_ph.base + (j - 1))
                        b = j % 2
                        if DBG_TRIG == 5:
                            s.copy(out=self.PONG[:, o, th * 1024:(th + 1) * 1024], in_=psm[:, n % 3, :])
                        s.activation(out=t1[:, b, :], in_=psm[:, n % 3, :], func=AF.Tanh,
                                     bias=bias_loc[:, 1, o:o + 1])
                        s.activation(out=t2[:, b, :], in_=t1[:, b, :], func=AF.Sin, scale=PI / 2)
                        s.activation(out=t3[:, b, :], in_=t1[:, b, :], func=AF.Sin,
                                     scale=-PI / 2, bias=self.ccol[:, 4:5])
                        self.sfence(s, s_ep)
                        n += 1
                        j += 1

            @block.vector
            def _(v):
                self.gate(v)
                j = 0
                for o in range(0 if amponly else KT):
                    for th in range(2):
                        v.wait_ge(s_ep.h, s_ep.base + (32 + j + 1))
                        b = j % 2
                        v.tensor_add(out=t4[:], in0=t2[:, b, :], in1=t3[:, b, :])
                        v.tensor_mul(out=t4[:], in0=t3[:, b, :], in1=t4[:])
                        v.tensor_scalar(out=t4[:], in0=t4[:], scalar1=2.0, scalar2=-1.0,
                                        op0=ALU.mult, op1=ALU.add)
                        sl = self.PONG[:, o, th * 1024:(th + 1) * 1024]
                        if DBG_TRIG == 5:
                            s_ph.inc(v.tensor_copy(out=t4[:, 0:4], in_=t4[:, 4:8]))
                        elif DBG_TRIG == 3:
                            s_ph.inc(v.tensor_copy(out=sl, in_=t2[:, j % 2, :]))
                        elif DBG_TRIG == 4:
                            s_ph.inc(v.tensor_copy(out=sl, in_=t3[:, j % 2, :]))
                        elif DBG_TRIG == 2:
                            s_ph.inc(v.tensor_copy(out=sl, in_=t1[:, j % 2, :]))
                        elif DBG_TRIG:
                            s_ph.inc(v.tensor_copy(out=sl, in_=t4[:]))
                        else:
                            s_ph.inc(v.tensor_tensor(out=sl, in0=sl, in1=t4[:], op=ALU.mult))
                        j += 1

            res = (s_ep, s_ep.n) if amponly else (s_ph, s_ph.n)
            self.set_gate(res)
            return res

    def mamba(self, src, dst):
        nc, block = self.nc, self.block
        NSEG = T // CSZ
        lnb = self.ln_boundary("m", src, None, prev=self.resid_done, dst_hbm=self.u2scr)
        if self.dump == "u2":
            self.dump_hbm_bf16(self.u2scr.rearrange("k p t -> (k p) t"), lnb)
            return True
        NJ = (PROJ + 64) // 128
        with contextlib.ExitStack() as ph:
            sb = lambda name, shape, dt=BF16: ph.enter_context(nc.sbuf_tensor(name, shape, dt))
            pt = lambda name, shape, dt=F32: ph.enter_context(nc.psum_tensor(name, shape, dt))
            u2T = sb("u2T", [128, 16, CSZ])
            ipT = sb("ipT", [128, 2, 16, 256])
            gateS = sb("gateS", [128, 32, CSZ])
            hbcS = sb("hbcS", [128, 48, CSZ + 4])
            dtp = sb("dtp", [64, 6, CSZ], F32)
            EbF = sb("EbF", [64, CSZ])
            mrawS = sb("mrawS", [64, CSZ], F32) if self.dump == "mdt" else None
            AcT = sb("AcT", [128, 2, 64], F32)
            nAcT = sb("nAcT", [128, 2, 64], F32)
            nAcL = sb("nAcL", [128, 2, 64], F32)
            alL = sb("alL", [64, 1], F32)
            dt2T = sb("dt2T", [128, 2, 64], F32)
            FT = sb("FT", [128, 2, 64], F32)
            hidT = sb("hidT", [128, 2, INTER])
            h2sc = sb("h2sc", [128, 2, 2, 2, 64])
            GmTS = sb("GmTS", [128, 8, 2, CSZ])
            exp1 = sb("exp1", [128, 2, 2, 2, CSZ])
            mT1 = sb("mT1", [128, 2, 2, 2, CSZ])
            sqt = sb("sqt", [128, 2, CSZ])
            attnF = sb("attnF", [128, 16, CSZ])
            opT = sb("opT", [128, 4, 4 * 128])
            xinR = sb("xinR", [128, 2, EMB], F32)
            hsel = sb("hsel", [64, 64, 128])
            hsl32 = sb("hsl32", [64, 2, 128], F32)
            cacc = sb("cacc", [128, 4, CSZ], F32)
            cwL = sb("cwL", [128, 48, 4], F32)
            cwN = sb("cwN", [48, 128, 4], F32)
            cbL = sb("cbL", [128, 48], F32)
            cbN = sb("cbN", [48, 128], F32)
            cbT = sb("cbT", [128, 48], F32)
            nwL = sb("nwL", [128, 32], F32)
            nwN = sb("nwN", [32, 128], F32)
            DcL = sb("DcL", [128, 32], F32)
            D2a = sb("D2a", [1, 32])
            D2b = sb("D2b", [1, 32])
            onesr = sb("onesr", [1, 128])
            dtbL = sb("dtbL", [64, 1], F32)
            dtbR = sb("dtbR", [64, 1], F32)
            AlgR = sb("AlgR", [64, 1], F32)
            AcolL = sb("AcolL", [64, 1], F32)
            vfm = sb("vfm", [128, 512], F32)
            mbias = sb("mbias", [128, 2, 256], F32)
            expin = sb("expin", [128, 2, 2, 256], F32)
            rrow = sb("rrow", [1, 4, CSZ], F32)
            psG = pt("psG", [128, 2, 512])
            psBT = pt("psBT", [128, 2, 512])
            psY = pt("psY", [128, 2, 512])
            psGm = pt("psGm", [128, 512])
            psRT = pt("psRT", [128, 1024], BF16)
            EsB = sb("EsB", [128, 2, 256])
            hf32 = sb("hf32", [128, 2, 256], F32)

            s_pre = self.sem("m_pre", persistent=True)
            s_preS = self.sem("m_preS", persistent=True)
            s_prd = self.dsem("m_prd")
            s_Dc = self.sem("m_Dc", persistent=True)

            @block.sync
            def _(sy):
                self.gate(sy)
                s_prd.inc(sy.dma_start(out=cwN[:], in_=self.inp["conv_w"].rearrange("(t p) k -> t p k", p=128)), 16)
                s_prd.inc(sy.dma_start(out=cbN[:], in_=self.inp["conv_b"].rearrange("(t p) -> t p", p=128)), 16)
                s_prd.inc(sy.dma_start(out=nwN[:], in_=self.inp["norm_w"].rearrange("(t p) -> t p", p=128)), 16)
                s_prd.inc(sy.dma_start(out=dtbR[:], in_=self.inp["dt_bias"].rearrange("(h o) -> h o", o=1)), 16)
                s_prd.inc(sy.dma_start(out=AlgR[:], in_=self.inp["A_log"].rearrange("(h o) -> h o", o=1)), 16)

            @block.vector
            def _(v):
                self.gate(v)
                v.memset(hsel[:], 1.0)
                v.memset(onesr[:], 1.0)
                v.memset(vfm[:], 0.0)
                for t in range(48):
                    v.memset(hbcS[:, t, 0:4], 0.0)
                s_pre.inc(v.memset(EbF[:], 0.0))

            def vpadm(v, n=2):
                for q in range(n):
                    v.tensor_copy(out=vfm[:, 256 + 128 * (q % 2):384 + 128 * (q % 2)],
                                  in_=vfm[:, 128 * (q % 2):128 + 128 * (q % 2)])

            s_prdS = self.dsem("m_prdS", sw=True)

            @block.gpsimd
            def _(g):
                self.gate(g)
                s_prdS.inc(g.dma_start(out=D2a[:], in_=self.inp["Dp"].rearrange("(t r) -> r t", r=2)[0:1, :], allow_slow_non_contiguous=True), 16)
                s_prdS.inc(g.dma_start(out=D2b[:], in_=self.inp["Dp"].rearrange("(t r) -> r t", r=2)[1:2, :], allow_slow_non_contiguous=True), 16)
                g.wait_ge(s_pre.h, s_pre.base + (1))
                s_pre.inc(g.affine_select(hsel[:], hsel[:], pattern=[[-1, 64], [0, 128]],
                                          compare_op=ALU.is_equal, fill=0.0, base=0,
                                          channel_multiplier=1))

            @block.scalar
            def _(s):
                self.gate(s)
                s.wait_ge(s_prd.h, s_prd.base + (16 * 5))
                s.activation(out=AcolL[:], in_=AlgR[:], func=AF.Exp)
                s.wait_ge(s_Dc.h, s_Dc.base + (2))
                s.copy(out=cbL[:], in_=psGm[:, 0:48])
                for _ in range(3):
                    s.copy(out=self.fence_scr[:, 512:1024], in_=self.fence_scr[:, 0:512])
                self.sfence(s, s_preS)

            @block.tensor
            def _(te):
                self.gate(te)
                te.wait_ge(s_pre.h, s_pre.base + (2))
                te.wait_ge(s_prdS.h, s_prdS.base + (32))
                te.matmul(psBT[0:64, 0, 0:32], onesr[0:1, 0:64], D2a[:], start=True, stop=True,
                          skip_group_check=True)
                te.matmul(psBT[64:128, 0, 0:32], onesr[0:1, 0:64], D2b[:], start=True,
                          stop=True, skip_group_check=True)
                te.wait_ge(s_prd.h, s_prd.base + (16 * 5))
                te.transpose(psGm[:, 0:48], cbN[:], self.ident32[0:48, 0:48])
                te.transpose(psGm[:, 48:80], nwN[:], self.ident32[0:32, 0:32])
                for k in range(4):
                    ins = te.transpose(psGm[:, 80 + 48 * k:128 + 48 * k], cwN[:, :, k],
                                       self.ident32[0:48, 0:48])
                s_Dc.inc(ins, 2)

            @block.vector
            def _(v):
                v.wait_ge(s_prd.h, s_prd.base + (16 * 5))
                v.wait_ge(s_Dc.h, s_Dc.base + (2))
                v.wait_ge(s_preS.h, s_preS.base + (1))
                for k in range(4):
                    v.tensor_copy(out=cwL[:, :, k], in_=psGm[:, 80 + 48 * k:128 + 48 * k])
                v.tensor_copy(out=nwL[:], in_=psGm[:, 48:80])
                v.tensor_copy(out=cbT[:], in_=psGm[:, 0:48])
                v.tensor_copy(out=DcL[:], in_=psBT[:, 0, 0:32])
                v.tensor_copy(out=dtbL[:], in_=dtbR[:])
                v.tensor_scalar_mul(out=AcolL[:], in0=AcolL[:], scalar1=-1.0)
                for jb in range(2):
                    v.tensor_scalar(out=mbias[:, jb, :], in0=self.tril01[jb][:],
                                    scalar1=1e9, scalar2=-1e9, op0=ALU.mult, op1=ALU.add)
                vpadm(v, 4)
                s_pre.inc(v.tensor_copy(out=vfm[:, 0:4], in_=vfm[:, 4:8]))

            ldone = lnb if isinstance(lnb, list) else [lnb]
            prev_pair = None
            all_st = []
            for sg in range(NSEG if self.upto == "full" or self.upto.startswith("ffn") or self.upto == "mamba" else 1):
                prev_pair = self._mamba_seg(sg, block, locals(), ldone, src, dst, prev_pair)
                if self._dumped:
                    return True
                all_st.append(prev_pair["st"])
            self.resid_done = all_st
            if self.dump == "x2":
                self.dump_tok(dst, all_st)
                return True
            self.set_gate(*all_st)
        return False

    def _mamba_seg(self, sg, block, L, ldone, src, dst, prev):
        nc = self.nc
        u2T, ipT, gateS, hbcS, dtp = L["u2T"], L["ipT"], L["gateS"], L["hbcS"], L["dtp"]
        EbF, AcT, nAcT, nAcL, alL = L["EbF"], L["AcT"], L["nAcT"], L["nAcL"], L["alL"]
        dt2T, FT, hidT, h2sc = L["dt2T"], L["FT"], L["hidT"], L["h2sc"]
        GmTS, exp1, mT1, sqt, attnF = L["GmTS"], L["exp1"], L["mT1"], L["sqt"], L["attnF"]
        opT, xinR, hsel, hsl32 = L["opT"], L["xinR"], L["hsel"], L["hsl32"]
        cacc, cwL, cbL, nwL, DcL, dtbL, AcolL = (L["cacc"], L["cwL"], L["cbL"],
                                                 L["nwL"], L["DcL"], L["dtbL"], L["AcolL"])
        vfm, rrow, mbias, expin = L["vfm"], L["rrow"], L["mbias"], L["expin"]
        mrawS = L["mrawS"]
        psG, psBT, psY, psGm, psRT, EsB = (L["psG"], L["psBT"], L["psY"],
                                           L["psGm"], L["psRT"], L["EsB"])
        hf32 = L["hf32"]
        s_pre, s_preS = L["s_pre"], L["s_preS"]
        TS = slice(sg * CSZ, (sg + 1) * CSZ)
        NJ = 81

        su2 = self.dsem(f"su2_{sg}")
        r_ipq = [self.dsem(f"rip_{sg}_0"), self.dsem(f"rip_{sg}_1")]
        s_gm = self.sem(f"sgm_{sg}")
        s_ev = self.sem(f"sevm_{sg}")
        s_cv = self.sem(f"scv_{sg}")
        s_sf = self.sem(f"ssf_{sg}")
        s_sg = self.sem(f"ssg_{sg}")
        s_dta = self.sem(f"sdta_{sg}")
        s_dtb = self.sem(f"sdtb_{sg}")
        s_dt = self.sem(f"sdt_{sg}")
        s_ex = self.sem(f"sex_{sg}")
        s_loc = self.sem(f"sloc_{sg}")
        s_tp2 = self.sem(f"stp2_{sg}")
        s_te = self.sem(f"ste_{sg}")
        s_gmm = self.sem(f"sgmm_{sg}")
        s_gme = self.sem(f"sgme_{sg}")
        s_ht = self.sem(f"sht_{sg}")
        s_hf = self.sem(f"shf_{sg}")
        s_he = self.sem(f"she_{sg}")
        s_hs = self.sem(f"shs_{sg}")
        s_bc2 = self.sem(f"sbc2_{sg}")
        s_x1f = self.sem(f"sx1f_{sg}")
        s_xi = self.sem(f"sxi_{sg}")
        s_m1 = self.sem(f"sm1_{sg}")
        s_ym = self.sem(f"sym_{sg}")
        s_ee = self.sem(f"see_{sg}")
        s_ep = self.sem(f"sep_{sg}")
        s_rq = self.sem(f"srq_{sg}")
        s_rr = self.sem(f"srr_{sg}")
        s_rsq = self.sem(f"srsq_{sg}")
        s_rb = self.sem(f"srb_{sg}")
        s_yn = self.sem(f"syn_{sg}")
        r_op = Rot(self, f"rop_{sg}", 4)
        s_om = self.sem(f"som_{sg}")
        s_oe = self.sem(f"soe_{sg}")
        r_xi = Rot(self, f"rxi_{sg}", 2)
        s_rt = self.sem(f"srt_{sg}")
        s_ra = self.sem(f"sra_{sg}")
        r_st = Rot(self, f"rstm_{sg}", 2)

        def pw(eng, key, n=None):
            if prev is None:
                return
            s_, t_ = prev[key]
            eng.wait_ge(s_.h, s_.base + (t_ if n is None else n))

        @block.sync
        def _(sy):
            if sg == 0:
                for s_, t_ in ldone:
                    sy.wait_ge(s_.h, s_.base + (t_))
                for nm in ("inproj", "outproj"):
                    cs, ct = self.cast_done[nm]
                    sy.wait_ge(cs.h, cs.base + (ct))
                sy.wait_ge(self.s_constd.h, self.s_constd.base + (48))
            pw(sy, "gm")
            for k in range(16):
                su2.inc(sy.dma_start(out=u2T[:, k, :], in_=self.u2scr[k, :, TS]), 16)
            for q in range(41):
                if q >= 2:
                    sy.wait_ge(s_gm.h, s_gm.base + (2 * (q - 1)))
                w = 256 if q < 40 else 128
                for d in range(16):
                    r_ipq[q % 2].inc(sy.dma_start(out=ipT[:, q % 2, d, 0:w],
                                                  in_=self.wb_inproj[q * 256:q * 256 + w, d * 128:(d + 1) * 128],
                                                  transpose=True), 16)
            if DBG_MLV < 4:
                return
            pw(sy, "om")
            for dg in range(8):
                for k in range(32):
                    n = dg * 32 + k
                    if n >= 4:
                        sy.wait_ge(s_om.h, s_om.base + (n - 3))
                    r_op.inc(sy.dma_start(out=opT[:, n % 4, 0:256],
                                          in_=self.wb_outproj[dg * 256:(dg + 1) * 256, k * 128:(k + 1) * 128],
                                          transpose=True), n)
            pw(sy, "ra")
            for i in range(2):
                r_xi.inc(sy.dma_start(out=xinR[:, i, :], in_=src[sg * 256 + i * 128:sg * 256 + (i + 1) * 128, :]), i)

        @block.tensor
        def _(te):
            pw(te, "ev")
            pw(te, "rt")
            pw(te, "ra")
            te.wait_ge(su2.h, su2.base + (16 * 16))
            for jt in range(NJ):
                q = jt // 2
                off = (jt - 2 * q) * 128
                te.wait_ge(r_ipq[q % 2].h, r_ipq[q % 2].base + (16 * 16 * (q // 2 + 1)))
                if jt >= 2:
                    te.wait_ge(s_ev.h, s_ev.base + (jt - 1))
                for d in range(16):
                    ins = te.matmul(psG[:, jt % 2, 0:256], ipT[:, q % 2, d, off:off + 128],
                                    u2T[:, d, :], start=(d == 0), stop=(d == 15))
                s_gm.inc(ins)
            if DBG_MLV < 2:
                return
            te.wait_ge(s_dt.h, s_dt.base + (1))
            pw(te, "te")
            for jb in range(2):
                ins = te.transpose(psBT[:, jb, 0:64], dtp[:, 5, jb * 128:(jb + 1) * 128], self.ident32[0:64, 0:64])
                ins = te.transpose(psBT[:, jb, 64:128], dtp[:, 3, jb * 128:(jb + 1) * 128], self.ident32[0:64, 0:64])
            s_tp2.inc(ins)
            te.wait_ge(s_ex.h, s_ex.base + (1))
            te.wait_ge(s_te.h, s_te.base + (1))
            for jb in range(2):
                ins = te.transpose(psBT[:, jb, 128:192], dtp[:, 2, jb * 128:(jb + 1) * 128], self.ident32[0:64, 0:64])
            s_tp2.inc(ins)
            if DBG_MLV < 2.2:
                return
            te.wait_ge(s_sf.h, s_sf.base + (12))
            te.wait_ge(s_ev.h, s_ev.base + (81))
            pw(te, "ym")
            for g in range(8):
                if g >= 1:
                    te.wait_ge(s_gme.h, s_gme.base + (g))
                for jb in range(2):
                    ins = te.matmul(psGm[:, jb * 256:(jb + 1) * 256], hbcS[:, 32 + g, jb * 128:(jb + 1) * 128],
                                    hbcS[:, 40 + g, 0:256], start=True, stop=True)
                s_gmm.inc(ins)
            if DBG_MLV < 2.3:
                return
            te.wait_ge(s_te.h, s_te.base + (3))
            pw(te, "he")
            n = 0
            for t in range(32):
                te.wait_ge(s_hf.h, s_hf.base + (t + 1))
                for jb in range(2):
                    if n >= 2:
                        te.wait_ge(s_he.h, s_he.base + (n - 1))
                    ins = te.transpose(psBT[:, n % 2, 0:128],
                                       hf32[:, t % 2, jb * 128:(jb + 1) * 128], self.ident32[:])
                    s_ht.inc(ins)
                    n += 1
            if DBG_MLV < 3:
                return
            te.wait_ge(s_gme.h, s_gme.base + (8))
            te.wait_ge(s_he.h, s_he.base + (64))
            te.wait_ge(s_te.h, s_te.base + (3))
            for h in range(64):
                hp, h2 = h // 2, h % 2
                te.wait_ge(s_hs.h, s_hs.base + (h + 1))
                if h >= 2:
                    te.wait_ge(s_xi.h, s_xi.base + (h - 1))
                ins = te.matmul(psBT[:, h % 2, 256:512], hsl32[:, h % 2, :], dtp[:, 5, :],
                                start=True, stop=True, skip_group_check=True)
                s_bc2.inc(ins)
                if h2 == 1:
                    te.wait_ge(s_m1.h, s_m1.base + (hp + 1))
                    if hp >= 1:
                        te.wait_ge(s_ee.h, s_ee.base + (hp))
                    if hp >= 2:
                        te.wait_ge(s_ep.h, s_ep.base + (hp - 1))
                    for hh in range(2):
                        hx = 2 * hp + hh
                        g = hx % 8
                        pb = slice(hh * 64, hh * 64 + 64)
                        for jb in range(2):
                            te.matmul(psY[pb, hp % 2, 0:256], hidT[:, jb, hx * 64:(hx + 1) * 64],
                                      mT1[:, hp % 2, hh, jb, :], start=(jb == 0), stop=(jb == 1),
                                      skip_group_check=True)
                        for jb in range(2):
                            te.matmul(psY[pb, hp % 2, 256:512], h2sc[:, hp % 2, hh, jb, :],
                                      GmTS[:, g, jb, :], start=(jb == 0), stop=(jb == 1),
                                      skip_group_check=True)
                    te.matmul(psGm[0:64, 0:256], hsel[:, 2 * hp, 0:64], EbF[:],
                              start=True, stop=True, skip_group_check=True)
                    ins = te.matmul(psGm[64:128, 0:256], hsel[:, 2 * hp + 1, 64:128], EbF[:],
                                    start=True, stop=True, skip_group_check=True)
                    s_ym.inc(ins)
                    if hp >= 1:
                        te.wait_ge(s_ep.h, s_ep.base + (hp))
                        s_rq.inc(te.matmul(psG[0:1, 0, 0:256], self.ones_col16[:, 0:1],
                                           sqt[:, (hp - 1) % 2, :], start=(hp == 1), stop=False,
                                           skip_group_check=True))
            te.wait_ge(s_ep.h, s_ep.base + (32))
            s_rq.inc(te.matmul(psG[0:1, 0, 0:256], self.ones_col16[:, 0:1], sqt[:, 31 % 2, :],
                               start=False, stop=True, skip_group_check=True))
            if DBG_MLV < 4:
                return
            te.wait_ge(s_rsq.h, s_rsq.base + (1))
            s_rb.inc(te.matmul(psBT[:, 0, 256:512], self.ones1_32[0:1, 0:128], rrow[0:1, 2, :],
                               start=True, stop=True, skip_group_check=True))
            for dg in range(8):
                for k in range(32):
                    n = dg * 32 + k
                    r_op.wait(te, n)
                    if k == 0:
                        te.wait_ge(s_yn.h, s_yn.base + (32))
                        if dg >= 1:
                            te.wait_ge(s_oe.h, s_oe.base + (2 * dg))
                    for dd in range(2):
                        ins = te.matmul(psG[:, dd, 0:256], opT[:, n % 4, dd * 128:(dd + 1) * 128],
                                        gateS[:, k, :], start=(k == 0), stop=(k == 31))
                    s_om.inc(ins)
            te.wait_ge(s_oe.h, s_oe.base + (16))
            for i in range(2):
                for w in range(2):
                    nw = 2 * i + w
                    if nw >= 1:
                        te.wait_ge(s_ra.h, s_ra.base + (nw))
                    for j in range(8):
                        t = w * 8 + j
                        ins = te.transpose(psRT[:, j * 128:(j + 1) * 128],
                                           attnF[:, t, i * 128:(i + 1) * 128], self.ident[:])
                    s_rt.inc(ins)

        def _ep(v, hp):
            v.wait_ge(s_ym.h, s_ym.base + (hp + 1))
            v.wait_ge(s_sg.h, s_sg.base + (4))
            if hp >= 2:
                v.wait_ge(s_rq.h, s_rq.base + (hp - 1))
            c = 2 * (hp % 2)
            v.tensor_mul(out=cacc[:, c, :], in0=EsB[:, hp % 2, :], in1=psY[:, hp % 2, 256:512])
            v.scalar_tensor_tensor(out=cacc[:, c + 1, :], in0=hbcS[:, hp, 0:256],
                                   scalar=DcL[:, hp:hp + 1], in1=cacc[:, c, :],
                                   op0=ALU.mult, op1=ALU.add)
            v.tensor_tensor(out=cacc[:, c, :], in0=cacc[:, c + 1, :], in1=psY[:, hp % 2, 0:256],
                            op=ALU.add)
            v.tensor_mul(out=gateS[:, hp, :], in0=gateS[:, hp, :], in1=cacc[:, c, :])
            s_ep.inc(v.tensor_mul(out=sqt[:, hp % 2, :], in0=gateS[:, hp, :], in1=gateS[:, hp, :]))

        @block.vector
        def _(v):
            def vp(n=2):
                for q in range(n):
                    v.tensor_copy(out=vfm[:, 256 + 128 * (q % 2):384 + 128 * (q % 2)],
                                  in_=vfm[:, 128 * (q % 2):128 + 128 * (q % 2)])
            pw(v, "yn")
            pw(v, "ep")
            pw(v, "gmm")
            pw(v, "bc2")
            if prev is None:
                v.wait_ge(s_pre.h, s_pre.base + (3))
            for jt in range(NJ):
                v.wait_ge(s_gm.h, s_gm.base + (jt + 1))
                if jt < 32:
                    s_ev.inc(v.tensor_copy(out=gateS[:, jt, :], in_=psG[:, jt % 2, 0:256]))
                elif jt < 80:
                    t = jt - 32
                    if sg > 0:
                        v.tensor_copy(out=hbcS[:, t, 0:3], in_=hbcS[:, t, 256:259])
                    s_ev.inc(v.tensor_copy(out=hbcS[:, t, 3:259], in_=psG[:, jt % 2, 0:256]))
                else:
                    if mrawS is not None:
                        v.tensor_copy(out=mrawS[:], in_=psG[0:64, jt % 2, 0:256])
                    s_ev.inc(v.tensor_scalar(out=dtp[:, 0, :], in0=psG[0:64, jt % 2, 0:256],
                                             scalar1=dtbL[:, 0:1], scalar2=None, op0=ALU.add))
            for tp in range(24):
                t0, t1 = 2 * tp, 2 * tp + 1
                a = 2 * (tp % 2)
                v.wait_ge(s_ev.h, s_ev.base + (33 + t1))
                if tp >= 2:
                    v.wait_ge(s_sf.h, s_sf.base + ((2 * tp - 3) // 4 + 1))
                v.tensor_scalar(out=cacc[:, a, :], in0=hbcS[:, t0, 0:256],
                                scalar1=cwL[:, t0, 0:1], scalar2=None, op0=ALU.mult)
                v.tensor_scalar(out=cacc[:, a + 1, :], in0=hbcS[:, t1, 0:256],
                                scalar1=cwL[:, t1, 0:1], scalar2=None, op0=ALU.mult)
                for k in range(1, 4):
                    v.scalar_tensor_tensor(out=cacc[:, a, :], in0=hbcS[:, t0, k:256 + k],
                                           scalar=cwL[:, t0, k:k + 1], in1=cacc[:, a, :],
                                           op0=ALU.mult, op1=ALU.add)
                    ins = v.scalar_tensor_tensor(out=cacc[:, a + 1, :], in0=hbcS[:, t1, k:256 + k],
                                                 scalar=cwL[:, t1, k:k + 1], in1=cacc[:, a + 1, :],
                                                 op0=ALU.mult, op1=ALU.add)
                s_cv.inc(ins, 2)
            v.tensor_scalar(out=dtp[:, 2, :], in0=dtp[:, 0, :], scalar1=0.0, scalar2=None, op0=ALU.max)
            v.tensor_scalar(out=dtp[:, 1, :], in0=dtp[:, 0, :], scalar1=-1.0, scalar2=None, op0=ALU.mult)
            vp(2)
            s_dta.inc(v.tensor_max(out=dtp[:, 1, :], in0=dtp[:, 1, :], in1=dtp[:, 0, :]))
            v.wait_ge(s_dtb.h, s_dtb.base + (1))
            v.tensor_add(out=dtp[:, 3, :], in0=dtp[:, 2, :], in1=dtp[:, 1, :])
            vp(2)
            v.tensor_scalar(out=dtp[:, 4, :], in0=dtp[:, 3, :], scalar1=AcolL[:, 0:1],
                            scalar2=None, op0=ALU.mult)
            vp(2)
            s_dt.inc(v.tensor_tensor_scan(out=dtp[:, 5, :], data0=dtp[:, 4, :], data1=dtp[:, 4, :],
                                          initial=0.0, op0=ALU.add, op1=ALU.bypass))
            if DBG_MLV < 2:
                return
            v.wait_ge(s_tp2.h, s_tp2.base + (1))
            pw(v, "loc")
            pw(v, "hs", 64)
            for jb in range(2):
                v.tensor_copy(out=AcT[:, jb, :], in_=psBT[:, jb, 0:64])
                v.tensor_scalar(out=nAcT[:, jb, :], in0=psBT[:, jb, 0:64], scalar1=-1.0,
                                scalar2=None, op0=ALU.mult)
                v.tensor_copy(out=dt2T[:, jb, :], in_=psBT[:, jb, 64:128])
            s_te.inc(v.tensor_copy(out=vfm[:, 0:4], in_=vfm[:, 4:8]))
            v.wait_ge(s_ex.h, s_ex.base + (1))
            s_te.inc(v.tensor_copy(out=EbF[:], in_=dtp[:, 0, :]))
            v.wait_ge(s_tp2.h, s_tp2.base + (2))
            for jb in range(2):
                ins = v.tensor_copy(out=FT[:, jb, :], in_=psBT[:, jb, 128:192])
            s_te.inc(ins)
            if DBG_MLV < 2.2:
                return
            for g in range(8):
                v.wait_ge(s_gmm.h, s_gmm.base + (g + 1))
                for jb in range(2):
                    ins = v.tensor_copy(out=GmTS[:, g, jb, :], in_=psGm[:, jb * 256:(jb + 1) * 256])
                s_gme.inc(ins)
            if DBG_MLV < 2.3:
                return
            for t in range(33):
                if t < 32:
                    if t >= 2:
                        v.wait_ge(s_ht.h, s_ht.base + (2 * t - 2))
                    s_hf.inc(v.tensor_copy(out=hf32[:, t % 2, :], in_=hbcS[:, t, 0:256]))
                if t >= 1:
                    tt_ = t - 1
                    for jb in range(2):
                        n = 2 * tt_ + jb
                        v.wait_ge(s_ht.h, s_ht.base + (n + 1))
                        v.tensor_scalar(out=hidT[:, jb, tt_ * 128:tt_ * 128 + 64],
                                        in0=psBT[:, n % 2, 0:64],
                                        scalar1=dt2T[:, jb, 2 * tt_:2 * tt_ + 1], scalar2=None, op0=ALU.mult)
                        s_he.inc(v.tensor_scalar(out=hidT[:, jb, tt_ * 128 + 64:(tt_ + 1) * 128],
                                                 in0=psBT[:, n % 2, 64:128],
                                                 scalar1=dt2T[:, jb, 2 * tt_ + 1:2 * tt_ + 2],
                                                 scalar2=None, op0=ALU.mult))
            if DBG_MLV < 3:
                return

            def _expin(hh_):
                for jb in range(2):
                    v.tensor_add(out=expin[:, hh_ % 2, jb, :], in0=psBT[:, hh_ % 2, 256:512],
                                 in1=mbias[:, jb, :])

            for h in range(64):
                hp, h2 = h // 2, h % 2
                if h >= 2:
                    v.wait_ge(s_bc2.h, s_bc2.base + (h - 1))
                s_hs.inc(v.tensor_copy(out=hsl32[:, h % 2, :], in_=hsel[:, h, :]))
                v.wait_ge(s_bc2.h, s_bc2.base + (h + 1))
                if h >= 2:
                    v.wait_ge(s_x1f.h, s_x1f.base + (h // 2))
                _expin(h)
                s_xi.inc(v.tensor_copy(out=vfm[:, 0:4], in_=vfm[:, 4:8]))
                if h2 == 1:
                    v.wait_ge(s_x1f.h, s_x1f.base + (hp + 1))
                    if hp >= 1:
                        v.wait_ge(s_ym.h, s_ym.base + (hp))
                    for hh in range(2):
                        g = (2 * hp + hh) % 8
                        for jb in range(2):
                            v.tensor_mul(out=mT1[:, hp % 2, hh, jb, :], in0=exp1[:, hp % 2, hh, jb, :],
                                         in1=GmTS[:, g, jb, :])
                    for hh in range(2):
                        hx = 2 * hp + hh
                        for jb in range(2):
                            ins = v.tensor_scalar(out=h2sc[:, hp % 2, hh, jb, :],
                                                  in0=hidT[:, jb, hx * 64:(hx + 1) * 64],
                                                  scalar1=FT[:, jb, hx:hx + 1], scalar2=None, op0=ALU.mult)
                    s_m1.inc(ins)
                    if hp >= 1:
                        v.wait_ge(s_ym.h, s_ym.base + (hp))
                        s_ee.inc(v.tensor_copy(out=EsB[:, (hp - 1) % 2, :], in_=psGm[:, 0:256]))
                        _ep(v, hp - 1)
            v.wait_ge(s_ym.h, s_ym.base + (32))
            s_ee.inc(v.tensor_copy(out=EsB[:, 31 % 2, :], in_=psGm[:, 0:256]))
            _ep(v, 31)
            if DBG_MLV < 4:
                return
            v.wait_ge(s_rq.h, s_rq.base + (32))
            v.tensor_scalar(out=rrow[0:1, 0, :], in0=psG[0:1, 0, 0:256], scalar1=1.0 / INTER,
                            scalar2=self.eps_rms, op0=ALU.mult, op1=ALU.add)
            vp(2)
            s_rr.inc(v.reciprocal(out=rrow[0:1, 1, :], in_=rrow[0:1, 0, :]))
            v.wait_ge(s_rb.h, s_rb.base + (1))
            for t in range(32):
                s_yn.inc(v.scalar_tensor_tensor(out=gateS[:, t, :], in0=gateS[:, t, :],
                                                scalar=nwL[:, t:t + 1], in1=psBT[:, 0, 256:512],
                                                op0=ALU.mult, op1=ALU.mult))
            for dg in range(8):
                v.wait_ge(s_om.h, s_om.base + (32 * (dg + 1)))
                for dd in range(2):
                    ins = v.tensor_copy(out=attnF[:, 2 * dg + dd, :], in_=psG[:, dd, 0:256])
                s_oe.inc(ins, 2)
            for i in range(2):
                r_xi.wait(v, i)
                for w in range(2):
                    nw = 2 * i + w
                    v.wait_ge(s_rt.h, s_rt.base + (nw + 1))
                    s_ra.inc(v.tensor_tensor(out=xinR[:, i, w * 1024:(w + 1) * 1024],
                                             in0=psRT[:, 0:1024],
                                             in1=xinR[:, i, w * 1024:(w + 1) * 1024], op=ALU.add))

        @block.scalar
        def _(s):
            pw(s, "ep")
            for t in range(48):
                s.wait_ge(s_cv.h, s_cv.base + (t + 1 + (t + 1) % 2))
                s.activation(out=hbcS[:, t, 0:256], in_=cacc[:, 2 * ((t // 2) % 2) + t % 2, :],
                             func=AF.Silu, bias=cbL[:, t:t + 1])
                if t % 4 == 3:
                    self.sfence(s, s_sf)
            if DBG_MLV >= 3:
                for t in range(32):
                    s.wait_ge(s_ev.h, s_ev.base + (t + 1))
                    s.activation(out=gateS[:, t, :], in_=gateS[:, t, :], func=AF.Silu)
                    if t % 8 == 7:
                        self.sfence(s, s_sg)
            s.wait_ge(s_dta.h, s_dta.base + (1))
            s.activation(out=dtp[:, 1, :], in_=dtp[:, 1, :], func=AF.Exp, scale=-1.0)
            s.activation(out=dtp[:, 1, :], in_=dtp[:, 1, :], func=AF.Ln, bias=self.ccol[0:64, 2:3])
            self.sfence(s, s_dtb)
            s.wait_ge(s_dt.h, s_dt.base + (1))
            s.copy(out=alL[:], in_=dtp[:, 5, 255:256])
            for _ in range(3):
                s.copy(out=self.fence_scr[:, 512:1024], in_=self.fence_scr[:, 0:512])
            s.activation(out=dtp[:, 0, :], in_=dtp[:, 5, :], func=AF.Exp)
            s.activation(out=dtp[:, 2, :], in_=dtp[:, 5, :], func=AF.Exp, scale=-1.0,
                         bias=alL[:, 0:1])
            self.sfence(s, s_ex)
            if DBG_MLV < 2:
                return
            s.wait_ge(s_te.h, s_te.base + (1))
            s.copy(out=nAcL[:, 0, :], in_=nAcT[:, 0, :])
            s.copy(out=nAcL[:, 1, :], in_=nAcT[:, 1, :])
            for _ in range(3):
                s.copy(out=self.fence_scr[:, 512:1024], in_=self.fence_scr[:, 0:512])
            self.sfence(s, s_loc)
            if DBG_MLV < 3:
                return
            s.wait_ge(s_loc.h, s_loc.base + (1))
            for h in range(64):
                hp, h2 = h // 2, h % 2
                if h2 == 0 and hp >= 2:
                    s.wait_ge(s_m1.h, s_m1.base + (hp - 1))
                s.wait_ge(s_xi.h, s_xi.base + (h + 1))
                for jb in range(2):
                    s.activation(out=exp1[:, hp % 2, h2, jb, :], in_=expin[:, h % 2, jb, :],
                                 func=AF.Exp, bias=nAcL[:, jb, h:h + 1])
                if h2 == 1:
                    self.sfence(s, s_x1f)
            if DBG_MLV < 4:
                return
            s.wait_ge(s_rr.h, s_rr.base + (1))
            s.activation(out=rrow[0:1, 2, :], in_=rrow[0:1, 1, :], func=AF.Sqrt)
            s.copy(out=rrow[0:1, 3, :], in_=rrow[0:1, 2, :])
            s.copy(out=rrow[0:1, 3, :], in_=rrow[0:1, 2, :])
            self.sfence(s, s_rsq)

        @block.sync
        def _(sy):
            if DBG_MLV < 4:
                return
            pw(sy, "st")
            for i in range(2):
                sy.wait_ge(s_ra.h, s_ra.base + (2 * i + 2))
                r_st.inc(sy.dma_start(out=dst[sg * 256 + i * 128:sg * 256 + (i + 1) * 128, :],
                                      in_=xinR[:, i, :]), i)

        done_st = [(r_st.sems[0], r_st.sems[0].s.n - r_st.sems[0].base),
                   (r_st.sems[1], r_st.sems[1].s.n - r_st.sems[1].base)]
        self._final_waits.extend(done_st)

        if sg == 0:
            if self.dump == "mdt":
                s_dd = self.dsem("sdd", sw=True)

                @block.gpsimd
                def _(g):
                    g.wait_ge(s_dt.h, s_dt.base + (1))
                    g.wait_ge(s_ex.h, s_ex.base + (1))
                    s_dd.inc(g.dma_start(out=self.dbg[0:64, 0:256], in_=dtp[:, 3, :]), 16)
                    s_dd.inc(g.dma_start(out=self.dbg[64:128, 0:256], in_=dtp[:, 5, :]), 16)
                    s_dd.inc(g.dma_start(out=self.dbg[128:192, 0:256], in_=dtp[:, 0, :]), 16)
                    s_dd.inc(g.dma_start(out=self.dbg[192:256, 0:256], in_=dtp[:, 2, :]), 16)
                    s_dd.inc(g.dma_start(out=self.dbg[256:320, 0:256], in_=mrawS[:]), 16)
                    s_dd.inc(g.dma_start(out=self.dbg[320:384, 0:256], in_=dtp[:, 1, :]), 16)
                self._final_waits.append((s_dd, s_dd.n))
                self._dumped = True
            if self.dump == "mhbc":
                s_dd = self.dsem("sdh", sw=True)

                @block.gpsimd
                def _(g):
                    g.wait_ge(s_sf.h, s_sf.base + (12))
                    for t in range(48):
                        s_dd.inc(g.dma_start(out=self.dbg[(t % 16) * 128:(t % 16 + 1) * 128,
                                                          (t // 16) * 256:(t // 16) * 256 + 256],
                                             in_=hbcS[:, t, 0:256]), 16)
                self._final_waits.append((s_dd, s_dd.n))
                self._dumped = True
            if self.dump == "myg":
                s_dd = self.dsem("sdy", sw=True)

                @block.gpsimd
                def _(g):
                    g.wait_ge(s_ep.h, s_ep.base + (32))
                    for t in range(32):
                        s_dd.inc(g.dma_start(out=self.dbg[(t % 16) * 128:(t % 16 + 1) * 128,
                                                          (t // 16) * 256:(t // 16) * 256 + 256],
                                             in_=gateS[:, t, :]), 16)
                self._final_waits.append((s_dd, s_dd.n))
                self._dumped = True
            if self.dump == "mattn":
                s_dd = self.dsem("sda", sw=True)

                @block.gpsimd
                def _(g):
                    g.wait_ge(s_oe.h, s_oe.base + (16))
                    for t in range(16):
                        s_dd.inc(g.dma_start(out=self.dbg[t * 128:(t + 1) * 128, 0:256],
                                             in_=attnF[:, t, :]), 16)
                self._final_waits.append((s_dd, s_dd.n))
                self._dumped = True
            if self._dumped:
                return {"st": done_st[0]}

        return {"gm": (s_gm, s_gm.n), "ev": (s_ev, s_ev.n), "cv": (s_cv, s_cv.n),
                "loc": (s_loc, s_loc.n),
                "ym": (s_ym, s_ym.n), "ep": (s_ep, s_ep.n), "yn": (s_yn, s_yn.n),
                "bc2": (s_bc2, s_bc2.n), "te": (s_te, s_te.n), "tp2": (s_tp2, s_tp2.n),
                "rt": (s_rt, s_rt.n), "gmm": (s_gmm, s_gmm.n), "he": (s_he, s_he.n),
                "hs": (s_hs, s_hs.n), "om": (s_om, s_om.n), "ra": (s_ra, s_ra.n),
                "st": done_st[0], "oe": (s_oe, s_oe.n)}

    def dump_fm(self, buf, done):
        nc, block = self.nc, self.block
        dl = done if isinstance(done, list) else [done]
        s_cp = self.sem("s_dcp")
        s_st = self.dsem("s_dst")
        with contextlib.ExitStack() as ph:
            tmp = ph.enter_context(nc.sbuf_tensor("dump_tmp", [128, 2, T], F32))

            @block.vector
            def _(v):
                self.gate(v)
                for sem, tgt in dl:
                    v.wait_ge(sem.h, sem.base + (tgt))
                for k in range(KT):
                    if k >= 2:
                        v.wait_ge(s_st.h, s_st.base + (16 * (k - 1)))
                    s_cp.inc(v.tensor_copy(out=tmp[:, k % 2, :], in_=buf[:, k, :]))

            @block.sync
            def _(sy):
                self.gate(sy)
                for k in range(KT):
                    sy.wait_ge(s_cp.h, s_cp.base + (k + 1))
                    s_st.inc(sy.dma_start(out=self.dbg[k * 128:(k + 1) * 128, :], in_=tmp[:, k % 2, :]), 16)

        self._final_waits.append((s_st, s_st.n))
        self._dumped = True

    def dump_hbm_bf16(self, src_hbm, done):
        nc, block = self.nc, self.block
        dl = done if isinstance(done, list) else [done]
        r_ld = Rot(self, "s_hbd", 2)
        s_cp = self.sem("s_hbc")
        s_st = self.dsem("s_hbs")
        with contextlib.ExitStack() as ph:
            tb = ph.enter_context(nc.sbuf_tensor("hbd_b", [128, 2, 2048], BF16))
            tf = ph.enter_context(nc.sbuf_tensor("hbd_f", [128, 2, 2048], F32))

            @block.sync
            def _(sy):
                self.gate(sy)
                for sem, tgt in dl:
                    sy.wait_ge(sem.h, sem.base + (tgt))
                for k in range(KT):
                    if k >= 2:
                        sy.wait_ge(s_cp.h, s_cp.base + (k - 1))
                    r_ld.inc(sy.dma_start(out=tb[:, k % 2, :], in_=src_hbm[k * 128:(k + 1) * 128, :]), k)

            @block.vector
            def _(v):
                self.gate(v)
                for k in range(KT):
                    r_ld.wait(v, k)
                    if k >= 2:
                        v.wait_ge(s_st.h, s_st.base + (16 * (k - 1)))
                    s_cp.inc(v.tensor_copy(out=tf[:, k % 2, :], in_=tb[:, k % 2, :]))

            @block.scalar
            def _(s):
                self.gate(s)
                for k in range(KT):
                    s.wait_ge(s_cp.h, s_cp.base + (k + 1))
                    s_st.inc(s.dma_start(out=self.dbg[k * 128:(k + 1) * 128, :], in_=tf[:, k % 2, :]), 16)

        self._final_waits.append((s_st, s_st.n))
        self._dumped = True

    def dump_tok(self, src_hbm, done):
        nc, block = self.nc, self.block
        dl = done if isinstance(done, list) else [done]
        r_cp = Rot(self, "s_dcp2", 2)
        s_st = self.dsem("s_dst2")
        with contextlib.ExitStack() as ph:
            tmp = ph.enter_context(nc.sbuf_tensor("dump_tmp2", [128, 2, EMB], F32))

            @block.sync
            def _(sy):
                self.gate(sy)
                for sem, tgt in dl:
                    sy.wait_ge(sem.h, sem.base + (tgt))
                for k in range(KT):
                    if k >= 2:
                        sy.wait_ge(s_st.h, s_st.base + (16 * (k - 1)))
                    r_cp.inc(sy.dma_start(out=tmp[:, k % 2, :], in_=src_hbm[k * 128:(k + 1) * 128, :]), k)

            @block.scalar
            def _(s):
                self.gate(s)
                for k in range(KT):
                    r_cp.wait(s, k)
                    s_st.inc(s.dma_start(out=self.dbg[k * 128:(k + 1) * 128, :], in_=tmp[:, k % 2, :]), 16)

        self._final_waits.append((s_st, s_st.n))
        self._dumped = True

    def finish(self):
        block = self.block
        waits = self._final_waits
        dma_shared = list(getattr(self, "_dma_shared", []))
        dma_extra = list(getattr(self, "_dma_extra", []))

        @block.gpsimd
        def _(g):
            for s, n in waits:
                g.wait_ge(s.h, s.base + (n))
            for sh in dma_shared:
                if sh.n:
                    g.wait_ge(sh.h, sh.n)
            for v in dma_extra:
                if v.s.n:
                    g.wait_ge(v.h, v.s.n)


def build(upto="full", dump=None):
    mk = MK(upto=upto, dump=dump)
    nc = mk.build()
    return nc


def kernel(**inputs):
    from concourse.bass_utils import run_bass_kernel_spmd

    x_all = np.ascontiguousarray(inputs["L0_attn_input_states"], dtype=np.float32)
    conv_w = np.ascontiguousarray(inputs["L0_attn_conv_w"].reshape(CONVD, 4))
    common = {
        "inproj_w": inputs["L0_attn_in_proj_w"],
        "conv_w": conv_w,
        "conv_b": inputs["L0_attn_conv_b"],
        "dt_bias": inputs["L0_attn_dt_bias"],
        "A_log": inputs["L0_attn_A_log"],
        "Dp": inputs["L0_attn_D"],
        "norm_w": inputs["L0_attn_norm_w"],
        "outproj_w": inputs["L0_attn_out_proj_w"],
        "amp_w": inputs["L0_ffn_amp_w"],
        "amp_b": inputs["L0_ffn_amp_b"],
        "ph_w": inputs["L0_ffn_ph_w"],
        "ph_b": inputs["L0_ffn_ph_b"],
        "ent_w": inputs["L0_ffn_ent_w"],
        "ent_b": inputs["L0_ffn_ent_b"],
        "ent_pw": inputs["L0_ffn_ent_pw"],
        "ent_pb": inputs["L0_ffn_ent_pb"],
        "path_w": inputs["L0_ffn_path_w"],
        "path_b": inputs["L0_ffn_path_b"],
        "phase": inputs["L0_ffn_phase"],
        "ln3_g": inputs["L0_ffn_ln3_g"],
        "ln3_b": inputs["L0_ffn_ln3_b"],
        "ln4_g": inputs["L0_ffn_ln4_g"],
        "ln4_b": inputs["L0_ffn_ln4_b"],
    }
    common = {k: np.ascontiguousarray(v, dtype=np.float32) for k, v in common.items()}
    n_cores = x_all.shape[0]
    in_maps = []
    for b in range(n_cores):
        m = dict(common)
        m["x"] = np.ascontiguousarray(x_all[b])
        in_maps.append(m)
    nc = build(upto="full", dump=None)
    res = run_bass_kernel_spmd(nc, in_maps, core_ids=list(range(n_cores)))
    out = np.stack([np.asarray(res.results[b]["out"], dtype=np.float32)
                    for b in range(n_cores)], axis=0)
    return out
